# version 30
# baseline (speedup 1.0000x reference)
import numpy as np
import ml_dtypes
import concourse.bass as bass
import concourse.mybir as mybir
from concourse.bass_utils import run_bass_kernel_spmd
from contextlib import ExitStack

BF16 = mybir.dt.bfloat16
F32 = mybir.dt.float32
AF = mybir.ActivationFunctionType
OP = mybir.AluOpType
BF = ml_dtypes.bfloat16

NT = 8
T = 512
KD = 8
CE = 16
CH = 24
MO = 8
NTOK = NT * T
HALO = 32
S127 = float(np.float32(1.0 / 127.0))
S127SQ = float(np.float32(1.0 / 127.0) * np.float32(1.0 / 127.0))
EPS = 1e-5

ENGINES = ("tensor", "vector", "scalar", "sync", "gpsimd")


class Sem:
    def __init__(self, handle):
        self.h = handle
        self.count = 0


class Sched:
    def __init__(self):
        self.ops = {e: [] for e in ENGINES}
        self.autosync = {}

    def add(self, eng, fn, waits=(), inc=None, inc_n=1):
        w = [(s, v) for (s, v) in waits if v > 0]
        aus = self.autosync.get(eng)
        if aus is not None:
            if aus.count > 0:
                w = w + [(aus, aus.count)]
            if inc is None:
                inc = aus
        incs = []
        val = None
        if inc is not None:
            inc.count += inc_n
            val = inc.count
            incs.append((inc, inc_n))
        self.ops[eng].append((w, fn, incs))
        return val


def _runner(sched, name):
    def go(eng):
        lastw = {}
        for waits, fn, incs in sched.ops[name]:
            for sem, val in waits:
                k = id(sem)
                if val > lastw.get(k, 0):
                    eng.wait_ge(sem.h, val)
                    lastw[k] = val
            inst = fn(eng)
            for sem, n in incs:
                inst.then_inc(sem.h, n)
    return go


def build_nc(debug=False):
    nc = bass.Bass("TRN2", target_bir_lowering=False)

    xTh = nc.dram_tensor("xTh", [1024, HALO + NTOK], F32, kind="ExternalInput").ap()
    xnat = nc.dram_tensor("xnat", [NTOK, 1024], F32, kind="ExternalInput").ap()
    win = nc.dram_tensor("win", [1024, 4096], F32, kind="ExternalInput").ap()
    wout = nc.dram_tensor("wout", [2048, 1024], F32, kind="ExternalInput").ap()
    wgt_d = nc.dram_tensor("wgt", [1024, 3072], BF16, kind="ExternalInput").ap()
    wut_d = nc.dram_tensor("wut", [1024, 3072], BF16, kind="ExternalInput").ap()
    wdt_d = nc.dram_tensor("wdt", [3072, 1024], BF16, kind="ExternalInput").ap()
    dcol_d = nc.dram_tensor("dcol", [128, CE], F32, kind="ExternalInput").ap()
    cw_d = nc.dram_tensor("cw", [128, CE * 4], F32, kind="ExternalInput").ap()
    cb_d = nc.dram_tensor("cb", [128, CE], F32, kind="ExternalInput").ap()
    lng_d = nc.dram_tensor("lng", [128, CE], F32, kind="ExternalInput").ap()
    lnb_d = nc.dram_tensor("lnb", [128, CE], F32, kind="ExternalInput").ap()
    rs_d = nc.dram_tensor("rs", [128, 1], F32, kind="ExternalInput").ap()
    out_d = nc.dram_tensor("out", [NTOK, 1024], F32, kind="ExternalOutput").ap()

    scr_kind = "ExternalOutput" if debug else "Internal"
    scr_xn = nc.dram_tensor("scr_xn", [NT, 128, KD * T], BF16, kind=scr_kind).ap()
    scr_yg = nc.dram_tensor("scr_yg", [NT, 128, CE * T], BF16, kind=scr_kind).ap()
    scr_h = nc.dram_tensor("scr_h", [NT, 128, CH * T], BF16, kind=scr_kind).ap()

    avgb = nc.alloc_sbuf_tensor("avgb", [128, 128], BF16)
    Db = nc.alloc_sbuf_tensor("Db", [128, CE * 128], BF16)
    D2b = nc.alloc_sbuf_tensor("D2b", [128, CE * 128], BF16)
    dcol = nc.alloc_sbuf_tensor("dcolS", [128, CE], F32)
    cw = nc.alloc_sbuf_tensor("cwS", [128, CE * 4], F32)
    cb = nc.alloc_sbuf_tensor("cbS", [128, CE], F32)
    lng = nc.alloc_sbuf_tensor("lngS", [128, CE], F32)
    lnb = nc.alloc_sbuf_tensor("lnbS", [128, CE], F32)
    gq = nc.alloc_sbuf_tensor("gq", [128, CE], F32)
    bq = nc.alloc_sbuf_tensor("bq", [128, CE], F32)
    rs = nc.alloc_sbuf_tensor("rsS", [128, 1], F32)
    epsc = nc.alloc_sbuf_tensor("epsc", [128, 1], F32)
    onec = nc.alloc_sbuf_tensor("onec", [128, 1], F32)
    s2c = nc.alloc_sbuf_tensor("s2c", [128, 1], F32)
    ARENA = 202816
    arena_buf = nc.alloc_sbuf_tensor("arena", [128, ARENA], mybir.dt.uint8)
    base = nc.lookup_mloc(arena_buf).addr

    def at(name, shape, dtype, off):
        nbytes = int(np.prod(shape[1:])) * (2 if dtype == BF16 else 4)
        assert off + nbytes <= ARENA, (name, off, nbytes)
        return nc.alloc_sbuf_tensor_at(name, list(shape), dtype, offset=base + off)

    xb = [at("xb0", [128, KD * T], BF16, 0),
          at("xb1", [128, KD * T], BF16, 8192)]
    x2f = at("x2f", [128, KD * T], BF16, 16384)
    xp = at("xp", [128, CE * (T + HALO)], BF16, 24576)
    xs2r = at("xs2r", [128, 2 * T], BF16, 41984)
    diag = at("diag", [128, CE * 4 * 128], BF16, 44032)
    mu1 = at("mu1", [128, T], BF16, 60416)
    r1 = at("r1", [128, T], BF16, 61440)
    tmf = at("tmf", [128, T], BF16, 62464)
    v1t = at("v1t", [128, T], F32, 63488)
    xn = [at("xn0", [128, KD * T], BF16, 65536),
          at("xn1", [128, KD * T], BF16, 73728)]
    winb = at("winb", [128, KD * 4096], BF16, 81920)
    xs = at("xs", [128, CE * T], BF16, 147456)
    sigz = at("sigz", [128, CE * T], BF16, 163840)
    yg = at("yg", [128, CE * T], BF16, 180224)
    tA = at("tA", [128, T], BF16, 196608)
    tB = at("tB", [128, T], BF16, 197632)
    mu2 = at("mu2", [128, T], BF16, 198656)
    r2 = at("r2", [128, T], BF16, 199680)
    sS1 = at("sS1", [128, T], F32, 200704)
    o = 188416
    def at2(name, shape, dtype):
        nonlocal o
        t = at(name, shape, dtype, o)
        o += int(np.prod(shape[1:])) * (2 if dtype == BF16 else 4)
        return t
    xbh = at2("xbh", [128, KD * HALO], BF16)
    xbh2 = at2("xbh2", [128, KD * HALO], BF16)
    xnh = at2("xnh", [128, KD * HALO], BF16)
    muh = at2("muh", [128, HALO], F32)
    varh = at2("varh", [128, HALO], F32)
    rh = at2("rh", [128, HALO], F32)
    sSh = at2("sSh", [128, HALO], F32)
    tmh = at2("tmh", [128, HALO], F32)
    identb = at2("identb", [128, 128], BF16)
    itmp = at2("itmp", [128, 128], BF16)
    wgt = at("wgtS", [128, KD * 3072], BF16, 0)
    xn2 = [at("xn2_0", [128, KD * T], BF16, 49152),
           at("xn2_1", [128, KD * T], BF16, 57344)]
    stg = [at("stg0", [128, 8 * 128], F32, 180224),
           at("stg1", [128, 8 * 128], F32, 184320)]
    wut = at("wutS", [128, KD * 3072], BF16, 65536)
    hbuf = at("hbuf", [128, CH * T], BF16, 116736)
    woutb = at("woutb", [128, CE * 1024], BF16, 0)
    wdt3 = at("wdt3", [128, 6 * 1024], BF16, 32768)
    yg3 = at("yg3", [128, CE * T], BF16, 45056)
    h3 = [at("h3_0", [128, CH * T], BF16, 141312),
          at("h3_1", [128, CH * T], BF16, 61440)]
    x3 = [at("x3_0", [128, 4 * 1024], F32, 86016),
          at("x3_1", [128, 4 * 1024], F32, 102400)]
    outb = at("outb", [128, 4 * 1024], F32, 118784)
    wdtp = [at(f"wdtp{j}", [128, 6 * 1024], BF16, 165888 + 12288 * j)
            for j in range(3)] + [wdt3]

    pp = [nc.alloc_psum_tensor(f"pp{j}", [128, 1024], F32) for j in range(4)]

    sched = Sched()
    es = ExitStack()

    def mksem(nm):
        return Sem(es.enter_context(nc.semaphore(nm)))
    s_t = mksem("s_t")
    s_v = mksem("s_v")
    s_s = mksem("s_s")
    s_g = mksem("s_g")
    s_cst = mksem("s_cst")
    s_halo = mksem("s_halo")
    s_winp = [mksem(f"s_winp{j}") for j in range(8)]
    s_wz = [mksem(f"s_wz{j}") for j in range(8)]
    s_xb = [mksem("s_xb0"), mksem("s_xb1")]
    s_xn2 = [mksem("s_xn2_0"), mksem("s_xn2_1")]
    s_yg3 = mksem("s_yg3")
    s_h3 = [mksem("s_h3_0"), mksem("s_h3_1")]
    s_x3 = [mksem("s_x3_0"), mksem("s_x3_1")]
    s_sxn = [mksem("s_sxn0"), mksem("s_sxn1")]
    s_syg = mksem("s_syg")
    s_shA = mksem("s_shA")
    s_shB = mksem("s_shB")
    s_sout = mksem("s_sout")
    s_dbg = mksem("s_dbg")
    s_wgp = [mksem(f"s_wgp{j}") for j in range(8)]
    s_wup = [mksem(f"s_wup{j}") for j in range(8)]
    s_wdp = [mksem(f"s_wdp{j}") for j in range(4)]
    s_wop = [mksem(f"s_wop{j}") for j in range(2)]
    sched.autosync = {"vector": s_v}

    A = sched.add

    for tile, valv in ((avgb, 1.0 / 1024.0), (epsc, EPS), (onec, 1.0), (s2c, S127SQ)):
        A("gpsimd", lambda e, t=tile, v=valv: e.memset(t.ap()[:, :], v), inc=s_g)
    pro_g = s_g.count
    A("gpsimd", lambda e: e.dma_start(
        xbh.ap().rearrange("p (k t) -> p k t", t=HALO),
        xTh[:, 0:HALO].rearrange("(k p) t -> p k t", p=128)), inc=s_halo, inc_n=16)
    A("gpsimd", lambda e: e.dma_start(
        xb[0].ap().rearrange("p (k t) -> p k t", k=KD),
        xTh[:, HALO: HALO + T].rearrange("(k p) t -> p k t", p=128)),
      inc=s_xb[0], inc_n=16)
    ms0_ld_xb0 = (s_xb[0], s_xb[0].count)
    A("gpsimd", lambda e: e.dma_start(
        xb[1].ap().rearrange("p (k t) -> p k t", k=KD),
        xTh[:, HALO + T: HALO + 2 * T].rearrange("(k p) t -> p k t", p=128)),
      inc=s_xb[1], inc_n=16)
    ms0_ld_xb1 = (s_xb[1], s_xb[1].count)
    A("gpsimd", lambda e: e.iota(itmp.ap()[:, :], [[1, 128]], channel_multiplier=-1,
                                 allow_small_or_imprecise_dtypes=True), inc=s_g)
    pro_g2 = s_g.count
    for name, dst, src in (
        ("dcol", dcol, dcol_d), ("cw", cw, cw_d), ("cb", cb, cb_d),
        ("lng", lng, lng_d), ("lnb", lnb, lnb_d), ("rs", rs, rs_d),
    ):
        A("sync", lambda e, d=dst, s=src: e.dma_start(d.ap()[:, :], s),
          inc=s_cst, inc_n=16)
    ld_cst = s_cst.count
    for j in range(4):
        A("gpsimd", lambda e, j=j: e.dma_start(
            winb.ap().rearrange("p (k n) -> p k n", k=KD)[:, :, j * 512:(j + 1) * 512],
            win[:, j * 512:(j + 1) * 512].rearrange("(k p) n -> p k n", p=128)),
          inc=s_winp[j], inc_n=16)

    def emit_wz_load(h):
        A("sync", lambda e, h=h: e.dma_start(
            stg[h % 2].ap().rearrange("p (k n) -> p k n", k=KD),
            win[:, 2048 + h * 128:2048 + (h + 1) * 128].rearrange(
                "(k p) n -> p k n", p=128)),
          waits=[(s_v, ms.get(("v_wz", h - 2), 0))] if h >= 2 else [],
          inc=s_wz[h % 8], inc_n=16)

    def emit_wz_cast(h):
        for k in range(KD):
            A("vector", lambda e, h=h, k=k: e.tensor_copy(
                winb.ap()[:, k * 4096 + 2048 + h * 128:k * 4096 + 2048 + (h + 1) * 128],
                stg[h % 2].ap()[:, k * 128:(k + 1) * 128]),
              waits=([(s_wz[h % 8], 16 * (h // 8 + 1))] if k == 0 else []), inc=s_v)
        ms[("v_wz", h)] = s_v.count

    def emit_const_derivs():
        A("vector", lambda e: e.tensor_single_scalar(identb.ap()[:, :], itmp.ap()[:, :],
                                                     0.0, OP.is_equal),
          waits=[(s_g, pro_g2), (s_cst, ld_cst)])
        pdc, dbp = [], []
        for c in range(CE):
            for k in range(4):
                A("vector", lambda e, c=c, k=k: e.tensor_scalar_mul(
                    diag.ap()[:, (c * 4 + k) * 128:(c * 4 + k + 1) * 128],
                    identb.ap()[:, :], cw.ap()[:, c * 4 + k:c * 4 + k + 1]))
            pdc.append(s_v.count)
            A("vector", lambda e, c=c: e.tensor_scalar(
                Db.ap()[:, c * 128:(c + 1) * 128], avgb.ap()[:, :],
                dcol.ap()[:, c:c + 1], 0.5, OP.mult, OP.mult),
              waits=[(s_g, pro_g)], inc=s_v)
            A("vector", lambda e, c=c: e.tensor_scalar_mul(
                D2b.ap()[:, c * 128:(c + 1) * 128],
                Db.ap()[:, c * 128:(c + 1) * 128], dcol.ap()[:, c:c + 1]), inc=s_v)
            dbp.append(s_v.count)
        A("vector", lambda e: e.tensor_scalar_mul(gq.ap()[:, :], lng.ap()[:, :], rs.ap()[:, :1]), inc=s_v)
        A("vector", lambda e: e.tensor_scalar_mul(bq.ap()[:, :], lnb.ap()[:, :], rs.ap()[:, :1]), inc=s_v)
        return pdc, dbp, s_v.count

    def ln_stats(engv, psum_pair, mu_sl, r_sl, n, wait_t_val, pre_v=()):
        tiny = True
        A("vector", lambda e: e.tensor_copy(mu_sl[:, 0:n], psum_pair.ap()[:, 0:n]),
          waits=[(s_t, wait_t_val)] + [w for w in pre_v if w[1] > 0], inc=s_v)
        cpv = s_v.count
        A("vector", lambda e: e.tensor_mul(tmf.ap()[:, 0:n] if n == T else tmh.ap()[:, 0:n],
                                           mu_sl[:, 0:n], mu_sl[:, 0:n]),
          waits=([(s_v, cpv)] if tiny else []), inc=s_v)
        mlv = s_v.count
        tm = tmf.ap() if n == T else tmh.ap()
        vt = v1t.ap() if n == T else varh.ap()
        sS = sS1.ap() if n == T else sSh.ap()
        A("vector", lambda e: e.tensor_sub(vt[:, 0:n], psum_pair.ap()[:, 512:512 + n], tm[:, 0:n]),
          waits=([(s_v, mlv)] if tiny else []), inc=s_v)
        v_done = s_v.count
        A("scalar", lambda e: e.activation(sS[:, 0:n], vt[:, 0:n], AF.Ln,
                                           bias=epsc.ap()[:, :1], scale=onec.ap()[:, :1]),
          waits=[(s_v, v_done), (s_g, pro_g)], inc=s_s)
        lnv = s_s.count
        A("scalar", lambda e: e.activation(r_sl[:, 0:n], sS[:, 0:n], AF.Exp, scale=-0.5),
          waits=([(s_s, lnv)] if tiny else []), inc=s_s)
        return v_done, s_s.count

    n = HALO
    A("vector", lambda e: e.tensor_mul(xbh2.ap()[:, :], xbh.ap()[:, :], xbh.ap()[:, :]),
      waits=[(s_halo, 16)], inc=s_v)
    v_x2h = s_v.count
    for k in range(KD):
        A("tensor", lambda e, k=k: e.matmul(
            pp[2].ap()[:, 0:n], avgb.ap()[:, :], xbh.ap()[:, k * n:(k + 1) * n],
            start=(k == 0), stop=(k == KD - 1)),
          waits=[(s_halo, 16), (s_g, pro_g)])
        A("tensor", lambda e, k=k: e.matmul(
            pp[2].ap()[:, 512:512 + n], avgb.ap()[:, :], xbh2.ap()[:, k * n:(k + 1) * n],
            start=(k == 0), stop=(k == KD - 1)),
          waits=[(s_v, v_x2h)], inc=(s_t if k == KD - 1 else None))
    t_stats_h = s_t.count
    v_done_h, s_rh = ln_stats("v", pp[2], muh.ap(), rh.ap(), HALO, t_stats_h)
    for k in range(KD):
        A("vector", lambda e, k=k: e.tensor_sub(
            tA.ap()[:, 0:n], xbh.ap()[:, k * n:(k + 1) * n], muh.ap()[:, 0:n]),
          waits=[(s_s, s_rh)], inc=s_v)
        sbv = s_v.count
        A("vector", lambda e, k=k: e.tensor_mul(
            xnh.ap()[:, k * n:(k + 1) * n], tA.ap()[:, 0:n], rh.ap()[:, 0:n]),
          waits=[(s_v, sbv)], inc=s_v)
    v_xnh = s_v.count
    ev_h = []
    for m in range(CE):
        for k in range(KD):
            A("tensor", lambda e, m=m, k=k: e.matmul(
                pp[3].ap()[:, (m % 2) * 512:(m % 2) * 512 + n],
                winb.ap()[:, k * 4096 + m * 128:k * 4096 + (m + 1) * 128],
                xnh.ap()[:, k * n:(k + 1) * n],
                start=(k == 0), stop=(k == KD - 1)),
              waits=([(s_v, v_xnh), (s_winp[m // 4], 16)] if k == 0 else [])
                    + ([(s_s, ev_h[m - 2])] if (k == 0 and m >= 2) else []),
              inc=(s_t if k == KD - 1 else None))
        t_m = s_t.count
        A("scalar", lambda e, m=m: e.copy(
            xp.ap()[:, m * (T + HALO):m * (T + HALO) + HALO],
            pp[3].ap()[:, (m % 2) * 512:(m % 2) * 512 + n]),
          waits=[(s_t, t_m)], inc=s_s)
        ev_h.append(s_s.count)

    if debug:
        for nm, buf, wl in (
            ("dbg_xbh", xbh, [(s_halo, 16)]),
            ("dbg_xbh2", xbh2, [(s_v, v_x2h)]),
            ("dbg_muh", muh, [(s_s, s_rh)]),
            ("dbg_varh", varh, [(s_s, s_rh)]),
            ("dbg_rh", rh, [(s_s, s_rh)]),
            ("dbg_xnh", xnh, [(s_v, v_xnh)]),
        ):
            shp = [buf.shape[0], buf.shape[1]]
            dt_ = nc.dram_tensor(nm, shp, buf.dtype, kind="ExternalOutput").ap()
            A("sync", lambda e, d=dt_, b=buf: e.dma_start(d, b.ap()[:, :]),
              waits=wl, inc=s_dbg, inc_n=16)

    ms = {}
    ms[("ld_xb", 0)] = ms0_ld_xb0
    ms[("ld_xb", 1)] = ms0_ld_xb1

    def M(name, i, *rest):
        return ms.get((name, i) + rest, 0)

    def MP(name, i, *rest):
        p = ms.get((name, i) + rest)
        return [p] if p else []

    def emit_ld_xb(j):
        if j >= NT:
            return
        if ("ld_xb", j) in ms:
            return
        A("gpsimd", lambda e, j=j: e.dma_start(
            xb[j % 2].ap().rearrange("p (k t) -> p k t", k=KD),
            xTh[:, HALO + j * T: HALO + (j + 1) * T].rearrange("(k p) t -> p k t", p=128)),
          waits=[(s_v, M("v_xn", j - 2))], inc=s_xb[j % 2], inc_n=16)
        ms[("ld_xb", j)] = (s_xb[j % 2], s_xb[j % 2].count)

    def emit_x2(j, ks=None):
        if j >= NT:
            return
        for k in (range(KD) if ks is None else ks):
            A("gpsimd", lambda e, k=k, j=j: e.tensor_mul(
                x2f.ap()[:, k * T:(k + 1) * T],
                xb[j % 2].ap()[:, k * T:(k + 1) * T], xb[j % 2].ap()[:, k * T:(k + 1) * T]),
              waits=[ms[("ld_xb", j)], (s_t, M("t_s1b", j - 1, k))], inc=s_g)
            ms[("g_x2", j, k)] = s_g.count

    def emit_s1k(j, k):
        if j >= NT:
            return
        A("tensor", lambda e, k=k, j=j: e.matmul(
            pp[0].ap()[:, 0:T], avgb.ap()[:, :], xb[j % 2].ap()[:, k * T:(k + 1) * T],
            start=(k == 0), stop=(k == KD - 1)),
          waits=([(s_s, M("s_r1", j - 1)), ms[("ld_xb", j)]] if k == 0 else []))
        A("tensor", lambda e, k=k: e.matmul(
            pp[0].ap()[:, 512:512 + T], avgb.ap()[:, :],
            x2f.ap()[:, k * T:(k + 1) * T],
            start=(k == 0), stop=(k == KD - 1)),
          waits=[(s_g, ms[("g_x2", j, k)])], inc=s_t)
        ms[("t_s1b", j, k)] = s_t.count
        if k == KD - 1:
            ms[("t_stats1", j)] = s_t.count

    def emit_stats1(j):
        for k in range(KD):
            emit_s1k(j, k)

    def vp1_vec(j):
        if j >= NT:
            return
        A("vector", lambda e: e.tensor_copy(mu1.ap()[:, :], pp[0].ap()[:, 0:T]),
          waits=[(s_t, ms[("t_stats1", j)])], inc=s_v)
        A("vector", lambda e: e.tensor_mul(tmf.ap()[:, :], mu1.ap()[:, :], mu1.ap()[:, :]), inc=s_v)
        A("vector", lambda e: e.tensor_sub(pp[0].ap()[:, 0:T], pp[0].ap()[:, 512:1024],
                                           tmf.ap()[:, :]), inc=s_v)
        ms[("v_var1", j)] = s_v.count

    def vp1_scl(j):
        if j >= NT:
            return
        A("scalar", lambda e: e.activation(sS1.ap()[:, :], pp[0].ap()[:, 0:T], AF.Ln,
                                           bias=epsc.ap()[:, :1], scale=onec.ap()[:, :1]),
          waits=[(s_v, ms[("v_var1", j)]), (s_g, pro_g)], inc=s_s)
        lnv = s_s.count
        A("scalar", lambda e: e.activation(r1.ap()[:, :], sS1.ap()[:, :], AF.Exp, scale=-0.5),
          waits=[(s_s, lnv), (s_v, M("v_xn", j - 1))], inc=s_s)
        ms[("s_r1", j)] = s_s.count

    def vp2_vec(j):
        A("vector", lambda e: e.tensor_copy(mu2.ap()[:, :], pp[2].ap()[:, 0:T]),
          waits=[(s_t, ms[("t_stats2", j)])], inc=s_v)
        A("vector", lambda e: e.tensor_mul(tmf.ap()[:, :], mu2.ap()[:, :], mu2.ap()[:, :]), inc=s_v)
        A("vector", lambda e: e.tensor_sub(pp[2].ap()[:, 0:T], pp[2].ap()[:, 512:1024],
                                           tmf.ap()[:, :]), inc=s_v)
        ms[("v_var2", j)] = s_v.count

    def vp2_scl(j):
        A("scalar", lambda e: e.activation(sS1.ap()[:, :], pp[2].ap()[:, 0:T], AF.Ln,
                                           bias=epsc.ap()[:, :1], scale=onec.ap()[:, :1]),
          waits=[(s_v, ms[("v_var2", j)]), (s_g, pro_g)], inc=s_s)
        ms[("s_lnv2", j)] = s_s.count
        lnv = s_s.count
        A("scalar", lambda e: e.activation(r2.ap()[:, :], sS1.ap()[:, :], AF.Exp, scale=-0.5),
          waits=[(s_s, lnv), (s_v, M("v_chainc", j - 1, CE - 1))], inc=s_s)
        ms[("s_r2", j)] = s_s.count

    def emit_xn_part(j, part):
        if j >= NT:
            return
        for k in range(2 * part, 2 * part + 2):
            A("vector", lambda e, k=k, j=j: e.tensor_sub(
                tA.ap()[:, :], xb[j % 2].ap()[:, k * T:(k + 1) * T], mu1.ap()[:, :]),
              waits=[(s_s, ms[("s_r1", j)])]
                    + ((MP("st_xn", j - 2) + [(s_t, M("t_pz", j - 2, 2 * CE - 1))])
                       if k == 2 * part else []))
            A("vector", lambda e, k=k, j=j: e.tensor_mul(
                xn[j % 2].ap()[:, k * T:(k + 1) * T], tA.ap()[:, :], r1.ap()[:, :]))
        if part == 3:
            ms[("v_xn", j)] = s_v.count
            A("sync", lambda e, j=j: e.dma_start(scr_xn[j], xn[j % 2].ap()[:, :]),
              waits=[(s_v, ms[("v_xn", j)])], inc=s_sxn[j % 2], inc_n=16)
            ms[("st_xn", j)] = (s_sxn[j % 2], s_sxn[j % 2].count)

    def emit_chain_c(j, c):
        A("vector", lambda e, c=c: e.scalar_tensor_tensor(
            tA.ap()[:, :], xs.ap()[:, c * T:(c + 1) * T], dcol.ap()[:, c:c + 1],
            mu2.ap()[:, :], OP.mult, OP.subtract),
          waits=[(s_s, ms[("s_silu", j, c)])])
        ms[("v_ch1", j, c)] = s_v.count
        A("vector", lambda e, c=c: e.scalar_tensor_tensor(
            tB.ap()[:, :], tA.ap()[:, :], gq.ap()[:, c:c + 1],
            r2.ap()[:, :], OP.mult, OP.mult),
          waits=([(s_s, ms[("s_r2", j)])] if c == 0 else []))
        A("vector", lambda e, c=c: e.scalar_tensor_tensor(
            yg.ap()[:, c * T:(c + 1) * T], tB.ap()[:, :], bq.ap()[:, c:c + 1],
            sigz.ap()[:, c * T:(c + 1) * T], OP.add, OP.mult),
          waits=(MP("st_yg", j - 1) if c == 0 else []))
        ms[("v_chainc", j, c)] = s_v.count
        if c == CE - 1:
            ms[("v_chain", j)] = s_v.count
            A("sync", lambda e, j=j: e.dma_start(scr_yg[j], yg.ap()[:, :]),
              waits=[(s_v, ms[("v_chain", j)])], inc=s_syg, inc_n=16)
            ms[("st_yg", j)] = (s_syg, s_syg.count)

    emit_wz_load(0)
    emit_wz_load(1)
    emit_x2(0)
    emit_stats1(0)
    vp1_vec(0)
    for h in (0, 1):
        emit_wz_cast(h)
        emit_wz_load(h + 2)
    vp1_scl(0)
    for p_ in range(4):
        emit_xn_part(0, p_)
    for h in (2, 3):
        emit_wz_cast(h)
        emit_wz_load(h + 2)
    emit_x2(1)
    emit_stats1(1)
    vp1_vec(1)
    for h in (4, 5, 6, 7):
        emit_wz_cast(h)
        emit_wz_load(h + 2)
    pro_diag_c, db_pair, pro_v = emit_const_derivs()
    pro_diag = pro_v
    for h in range(8, 16):
        emit_wz_cast(h)
        if h + 2 < 16:
            emit_wz_load(h + 2)

    for i in range(NT):
        sl = i % 2
        emit_ld_xb(i + 2)

        if i > 0:
            vp2_scl(i - 1)
        vp1_scl(i + 1)

        if i > 0:
            A("vector", lambda e: e.tensor_copy(
                xp.ap().rearrange("p (c w) -> p c w", c=CE)[:, :, HALO - 8:HALO],
                xp.ap().rearrange("p (c w) -> p c w", c=CE)[:, :, HALO + T - 8:HALO + T]),
              waits=[(s_s, M("s_evac", i - 1, 15)), (s_t, M("t_conv", i - 1, CE - 1))])
            ms[("v_halo", i)] = s_v.count

        def emit_sigz(c):
            src_pp = pp[1] if c % 4 < 2 else pp[3]
            half = (c % 4) % 2
            if c < 2:
                A("scalar", lambda e, c=c, src_pp=src_pp, half=half: e.copy(
                    sigz.ap()[:, c * T:(c + 1) * T],
                    src_pp.ap()[:, half * 512:(half + 1) * 512]),
                  waits=[(s_t, ms[("t_pz", i, CE + c)]),
                         (s_v, M("v_chainc", i - 1, c))], inc=s_s)
            else:
                A("scalar", lambda e, c=c, src_pp=src_pp, half=half: e.activation(
                    sigz.ap()[:, c * T:(c + 1) * T],
                    src_pp.ap()[:, half * 512:(half + 1) * 512], AF.Sigmoid),
                  waits=[(s_t, ms[("t_pz", i, CE + c)]),
                         (s_v, M("v_chainc", i - 1, c))], inc=s_s)
            ms[("s_sigz", i, c)] = s_s.count

        def emit_sigz_ip(c):
            A("scalar", lambda e, c=c: e.activation(
                sigz.ap()[:, c * T:(c + 1) * T],
                sigz.ap()[:, c * T:(c + 1) * T], AF.Sigmoid),
              waits=[(s_s, ms[("s_sigz", i, c)])], inc=s_s)

        def emit_conv(c):
            base = c * (T + HALO) + (HALO - 3)
            w0 = [(s_s, ms[("s_evac", i, c)]),
                  (s_s, (M("s_silu", i, c - 2) if c >= 2 else
                         (M("s_sigz", i - 1, 14 + c) if i > 0 else ev_h[14 + c]))),
                  (s_v, pro_diag_c[c] if i == 0 else pro_diag)]
            if i > 0:
                w0 = w0 + [(s_v, M("v_halo", i))]
            for k in range(4):
                A("tensor", lambda e, c=c, k=k, base=base: e.matmul(
                    pp[3].ap()[:, (c % 2) * 512:(c % 2 + 1) * 512],
                    diag.ap()[:, (c * 4 + k) * 128:(c * 4 + k + 1) * 128],
                    xp.ap()[:, base + k:base + k + T],
                    start=(k == 0), stop=(k == 3)),
                  waits=w0 if k == 0 else [],
                  inc=(s_t if k == 3 else None))
            ms[("t_conv", i, c)] = s_t.count

        def emit_silu(c):
            A("scalar", lambda e, c=c: e.activation(
                xs.ap()[:, c * T:(c + 1) * T],
                pp[3].ap()[:, (c % 2) * 512:(c % 2 + 1) * 512],
                AF.Silu, bias=cb.ap()[:, c:c + 1], scale=onec.ap()[:, :1]),
              waits=[(s_t, ms[("t_conv", i, c)]),
                     (s_v, M("v_ch1", i - 1, c))], inc=s_s)
            ms[("s_silu", i, c)] = s_s.count

        def emit_xs2(c):
            A("gpsimd", lambda e, c=c: e.tensor_mul(
                xs2r.ap()[:, (c % 2) * T:(c % 2 + 1) * T],
                xs.ap()[:, c * T:(c + 1) * T], xs.ap()[:, c * T:(c + 1) * T]),
              waits=[(s_s, ms[("s_silu", i, c)]), (s_t, M("t_s2b", i, c - 2))],
              inc=s_g)
            ms[("g_xs2", i, c)] = s_g.count

        def emit_sv(c):
            A("tensor", lambda e, c=c: e.matmul(
                pp[2].ap()[:, 0:T], Db.ap()[:, c * 128:(c + 1) * 128],
                xs.ap()[:, c * T:(c + 1) * T], start=(c == 0), stop=(c == CE - 1)),
              waits=[(s_s, ms[("s_silu", i, c)]),
                     (s_v, db_pair[c] if i == 0 else 0)]
                    + (([(s_s, M("s_lnv2", i - 1))] if i > 0 else [(s_v, v_done_h)])
                       if c == 0 else []))
            A("tensor", lambda e, c=c: e.matmul(
                pp[2].ap()[:, 512:512 + T], D2b.ap()[:, c * 128:(c + 1) * 128],
                xs2r.ap()[:, (c % 2) * T:(c % 2 + 1) * T],
                start=(c == 0), stop=(c == CE - 1)),
              waits=[(s_g, ms[("g_xs2", i, c)])], inc=s_t)
            ms[("t_s2b", i, c)] = s_t.count
            if c == CE - 1:
                ms[("t_stats2", i)] = s_t.count

        def zbank(c):
            return (pp[1] if c % 4 < 2 else pp[3]), (c % 4) % 2
        for m in range(2 * CE):
            if m in (20, 21):
                emit_sigz_ip(m - 20)
            if 0 <= m - 18 < CE - 2:
                emit_sigz(m - 18)
            first_w = [(s_v, ms[("v_xn", i)])] if m == 0 else []
            if m < 2:
                bank_w = [(s_s, M("s_sigz", i - 1, 12 + m))] if i > 0 else []
            elif m < CE + 2:
                bank_w = [(s_s, M("s_evac", i, m - 2))]
            else:
                c = m - 16
                if c < 4:
                    bank_w = [(s_s, ms[("s_silu", i, 12 + c)])]
                else:
                    bank_w = [(s_s, M("s_sigz", i, c - 4))]
            if m < CE:
                dst, half = pp[1], m % 2
            else:
                dst, half = zbank(m - 16)
            for k in range(KD):
                A("tensor", lambda e, m=m, k=k, sl=sl, dst=dst, half=half: e.matmul(
                    dst.ap()[:, half * 512:(half + 1) * 512],
                    winb.ap()[:, k * 4096 + m * 128:k * 4096 + (m + 1) * 128],
                    xn[sl].ap()[:, k * T:(k + 1) * T],
                    start=(k == 0), stop=(k == KD - 1)),
                  waits=(first_w + bank_w
                         + ([(s_winp[m // 4], 16)] if m < 16 else
                            [(s_v, ms[("v_wz", m - 16)])])) if k == 0 else [],
                  inc=(s_t if k == KD - 1 else None))
            ms[("t_pz", i, m)] = s_t.count
            if m < CE:
                hw = [(s_v, M("v_halo", i))] if i > 0 else []
                A("scalar", lambda e, m=m: e.copy(
                    xp.ap()[:, m * (T + HALO) + HALO:(m + 1) * (T + HALO)],
                    pp[1].ap()[:, (m % 2) * 512:(m % 2 + 1) * 512]),
                  waits=[(s_t, ms[("t_pz", i, m)]), (s_t, M("t_conv", i - 1, m))] + hw,
                  inc=s_s)
                ms[("s_evac", i, m)] = s_s.count
            if 0 <= m - 2 < CE:
                emit_conv(m - 2)
                emit_silu(m - 2)
            if 0 <= m - 3 < CE:
                emit_xs2(m - 3)
            if 10 <= m <= 17:
                emit_x2(i + 2, [m - 10])
            if i > 0 and m < CE:
                emit_chain_c(i - 1, m)
            if 0 <= m - 4 < CE:
                emit_sv(m - 4)
            if 17 <= m <= 20:
                emit_xn_part(i + 1, m - 17)
            if 0 <= m - 12 < KD:
                emit_s1k(i + 2, m - 12)
            if m == 21:
                vp2_vec(i)
            if m == 23:
                vp1_vec(i + 2)
        emit_sigz(14)
        emit_sigz(15)
        if debug and i == 0:
            for nm, buf, wl in (
                ("dbg_xn", xn[0], [(s_v, ms[("v_xn", 0)])]),
                ("dbg_xp", xp, [(s_s, ms[("s_evac", 0, CE - 1)])]),
                ("dbg_sigz", sigz, [(s_s, ms[("s_sigz", 0, CE - 1)])]),
                ("dbg_xs", xs, [(s_s, ms[("s_silu", 0, CE - 1)])]),
                ("dbg_yg", yg, [(s_v, ms[("v_chain", 0)])]),
            ):
                shp = [buf.shape[0], buf.shape[1]]
                dt_ = nc.dram_tensor(nm, shp, buf.dtype, kind="ExternalOutput").ap()
                A("sync", lambda e, d=dt_, b=buf: e.dma_start(d, b.ap()[:, :]),
                  waits=wl, inc=s_dbg, inc_n=16)

    p1_pz_done = ms[("t_pz", NT - 1, 2 * CE - 1)]
    p1_conv_done = ms[("t_conv", NT - 1, CE - 1)]
    p1_s2b_done = ms[("t_s2b", NT - 1, CE - 1)]
    p1_evac_done = ms[("s_evac", NT - 1, CE - 1)]
    p1_sigz_done = ms[("s_sigz", NT - 1, CE - 1)]
    p1_var2_done = ms[("v_var2", NT - 1)]
    NP2 = 384
    E1 = [(s_t, p1_s2b_done), (s_s, p1_evac_done)]
    E2 = [(s_t, p1_conv_done), (s_v, p1_var2_done)]
    A("sync", lambda e: e.dma_start(xn2[0].ap()[:, :], scr_xn[0]),
      waits=E2 + [(s_sxn[0], 16)], inc=s_xn2[0], inc_n=16)
    ms[("ld_xn2", 0)] = (s_xn2[0], s_xn2[0].count)
    for j in range(8):
        A("gpsimd", lambda e, j=j: e.dma_start(
            wgt.ap().rearrange("p (k n) -> p k n", k=KD)[:, :, j * NP2:(j + 1) * NP2],
            wgt_d[:, j * NP2:(j + 1) * NP2].rearrange("(k p) n -> p k n", p=128)),
          waits=E1, inc=s_wgp[j], inc_n=16)
    for j in range(8):
        A("gpsimd", lambda e, j=j: e.dma_start(
            wut.ap().rearrange("p (k n) -> p k n", k=KD)[:, :, j * NP2:(j + 1) * NP2],
            wut_d[:, j * NP2:(j + 1) * NP2].rearrange("(k p) n -> p k n", p=128)),
          waits=[(s_t, p1_pz_done)] + MP("st_xn", NT - 1), inc=s_wup[j], inc_n=16)
    vp2_scl(NT - 1)
    p1_r2_done = ms[("s_r2", NT - 1)]

    first2 = {"tensor": [(s_s, ms[("s_r1", NT - 1)])],
              "tensor_u": [(s_s, p1_sigz_done)],
              "vector": [], "scalar": [], "sync": []}

    def emit_sg(i, m):
        stw = [(s_shA, 16 * i)] if m < 12 else [(s_shB, 16 * i)]
        A("scalar", lambda e, m=m: e.activation(
            hbuf.ap()[:, m * T:(m + 1) * T],
            pp[0].ap()[:, (m % 2) * 512:(m % 2 + 1) * 512], AF.Silu, scale=S127),
          waits=first2["scalar"] + [(s_t, ms[("t_g", i, m)])]
                + [(s, v) for (s, v) in stw if v > 0], inc=s_s)
        first2["scalar"] = []
        ms[("s_sg", i, m)] = s_s.count

    def emit_vh(i, m):
        A("vector", lambda e, m=m: e.scalar_tensor_tensor(
            hbuf.ap()[:, m * T:(m + 1) * T], hbuf.ap()[:, m * T:(m + 1) * T],
            s2c.ap()[:, :1], pp[1].ap()[:, (m % 2) * 512:(m % 2 + 1) * 512],
            OP.mult, OP.mult),
          waits=first2["vector"] + [(s_s, ms[("s_sg", i, m)]), (s_t, ms[("t_u", i, m)])]
                + ([(s_v, M("v_chainc", NT - 1, m - 8))]
                   if (i == 0 and m >= 8) else []),
          inc=s_v)
        first2["vector"] = []
        ms[("v_h", i, m)] = s_v.count

    for i in range(NT):
        sl = i % 2
        if i > 0:
            A("gpsimd", lambda e, i=i, sl=sl: e.dma_start(
                xn2[sl].ap()[:, :], scr_xn[i]),
              waits=[(s_t, M("t_u", i - 2, CH - 1)), (s_sxn[sl], 16 * (i // 2 + 1))]
                    + (E2 if i < 2 else []),
              inc=s_xn2[sl], inc_n=16)
            ms[("ld_xn2", i)] = (s_xn2[sl], s_xn2[sl].count)

        if 2 <= i <= 4:
            jj = i - 2
            E4 = [(s_v, M("v_chain", NT - 1)), (s_syg, 16 * NT)] \
                 + ([(s_s, p1_r2_done)] if jj == 2 else [])
            A("gpsimd", lambda e, jj=jj: e.dma_start(
                wdtp[jj].ap()[:, :],
                wdt_d[jj * 768:(jj + 1) * 768, :].rearrange("(k p) n -> p k n", p=128)),
              waits=E4, inc=s_wdp[jj], inc_n=16)
            if jj == 0:
                A("scalar", lambda e: e.dma_start(
                    h3[0].ap()[:, :], scr_h[0]),
                  waits=[(s_v, M("v_chain", NT - 1)), (s_syg, 16 * NT),
                         (s_shA, 16), (s_shB, 16)],
                  inc=s_h3[0], inc_n=16)
                ms[("ld_h3", 0)] = (s_h3[0], s_h3[0].count)

        def emit_gate(m):
            gw = ([ms[("ld_xn2", i)]] if m == 0 else []) + [(s_wgp[m // 3], 16)] \
                 + [(s_s, M("s_sg", i, m - 2) if m >= 2 else M("s_sg", i - 1, CH - 2 + m))]
            for k in range(KD):
                A("tensor", lambda e, m=m, k=k, sl=sl: e.matmul(
                    pp[0].ap()[:, (m % 2) * 512:(m % 2 + 1) * 512],
                    wgt.ap()[:, k * 3072 + m * 128:k * 3072 + (m + 1) * 128],
                    xn2[sl].ap()[:, k * T:(k + 1) * T],
                    start=(k == 0), stop=(k == KD - 1)),
                  waits=(first2["tensor"] + gw) if k == 0 else [],
                  inc=(s_t if k == KD - 1 else None))
            first2["tensor"] = []
            ms[("t_g", i, m)] = s_t.count

        def emit_up(m):
            uw = ((first2["tensor_u"] if i == 0 else []) if m == 0 else []) \
                 + [(s_wup[m // 3], 16)] \
                 + [(s_v, M("v_h", i, m - 2) if m >= 2 else M("v_h", i - 1, CH - 2 + m))]
            for k in range(KD):
                A("tensor", lambda e, m=m, k=k, sl=sl: e.matmul(
                    pp[1].ap()[:, (m % 2) * 512:(m % 2 + 1) * 512],
                    wut.ap()[:, k * 3072 + m * 128:k * 3072 + (m + 1) * 128],
                    xn2[sl].ap()[:, k * T:(k + 1) * T],
                    start=(k == 0), stop=(k == KD - 1)),
                  waits=uw if k == 0 else [], inc=(s_t if k == KD - 1 else None))
            ms[("t_u", i, m)] = s_t.count

        if i == 0 or i == NT - 1:
            for m in range(CH):
                if i == 0 and m < CE:
                    emit_chain_c(NT - 1, m)
                emit_gate(m)
                emit_sg(i, m)
            for m in range(CH):
                emit_up(m)
                emit_vh(i, m)
                if m == 11:
                    A("sync", lambda e, i=i: e.dma_start(
                        scr_h[i][:, 0:12 * T], hbuf.ap()[:, 0:12 * T]),
                      waits=[(s_v, ms[("v_h", i, 11)])], inc=s_shA, inc_n=16)
                    ms[("st_hA", i)] = (s_shA, s_shA.count)
            A("sync", lambda e, i=i: e.dma_start(
                scr_h[i][:, 12 * T:], hbuf.ap()[:, 12 * T:]),
              waits=[(s_v, ms[("v_h", i, CH - 1)])], inc=s_shB, inc_n=16)
            ms[("st_hB", i)] = (s_shB, s_shB.count)
            continue

        for m in range(CH):
            emit_gate(m)
            emit_up(m)
            emit_sg(i, m)
            emit_vh(i, m)
            if m == 11:
                A("sync", lambda e, i=i: e.dma_start(
                    scr_h[i][:, 0:12 * T], hbuf.ap()[:, 0:12 * T]),
                  waits=[(s_v, ms[("v_h", i, 11)])], inc=s_shA, inc_n=16)
                ms[("st_hA", i)] = (s_shA, s_shA.count)
        A("sync", lambda e, i=i: e.dma_start(
            scr_h[i][:, 12 * T:], hbuf.ap()[:, 12 * T:]),
          waits=[(s_v, ms[("v_h", i, CH - 1)])], inc=s_shB, inc_n=16)
        ms[("st_hB", i)] = (s_shB, s_shB.count)

    p2_g_done = ms[("t_g", NT - 1, CH - 1)]
    p2_u_done = ms[("t_u", NT - 1, CH - 1)]
    p2_h_done = ms[("v_h", NT - 1, CH - 1)]
    p2_sg_done = ms[("s_sg", NT - 1, CH - 1)]
    shA_total = s_shA.count
    shB_total = s_shB.count
    syg_total = s_syg.count

    for j in range(2):
        A("gpsimd", lambda e, j=j: e.dma_start(
            woutb.ap().rearrange("p (k n) -> p k n", k=CE)[:, j * 8:(j + 1) * 8, :],
            wout[j * 1024:(j + 1) * 1024, :].rearrange("(k p) n -> p k n", p=128)),
          waits=[(s_t, p2_g_done)], inc=s_wop[j], inc_n=16)
    A("gpsimd", lambda e: e.dma_start(
        wdtp[3].ap()[:, :],
        wdt_d[3 * 768:4 * 768, :].rearrange("(k p) n -> p k n", p=128)),
      waits=[(s_t, p2_g_done)], inc=s_wdp[3], inc_n=16)

    first3 = {"tensor": [(s_s, p2_sg_done), (s_v, p2_h_done)],
              "vector": [(s_shA, shA_total), (s_shB, shB_total)], "sync": []}

    for i in range(NT):
        sl = i % 2
        if i > 0:
            A("gpsimd", lambda e, i=i, sl=sl: e.dma_start(h3[sl].ap()[:, :], scr_h[i]),
              waits=[(s_t, M("t_down", i - 2, 3)),
                     (s_shA, min(16 * (i + 1), shA_total)), (s_shB, 16 * (i + 1))]
                    + ([(s_t, p2_u_done)] if i < 3 else []),
              inc=s_h3[sl], inc_n=16)
            ms[("ld_h3", i)] = (s_h3[sl], s_h3[sl].count)
        A("gpsimd", lambda e, i=i, sl=sl: e.dma_start(
            x3[sl].ap().rearrange("p (s d) -> p s d", s=4),
            xnat[i * T:(i + 1) * T, :].rearrange("(s p) d -> p s d", p=128)),
          waits=[(s_v, M("v_evac3", i - 2, 3))]
                + ([[(s_t, p2_u_done)], [(s_v, p2_h_done), (s_shA, shA_total)]][i]
                   if i < 2 else []),
          inc=s_x3[sl], inc_n=16)
        ms[("ld_x3", i)] = (s_x3[sl], s_x3[sl].count)
        A("gpsimd", lambda e, i=i: e.dma_start(yg3.ap()[:, :], scr_yg[i]),
          waits=[(s_t, M("t_acc", i - 1, 3)), (s_syg, 16 * (i + 1))]
                + ([(s_t, p2_u_done)] if i == 0 else []),
          inc=s_yg3, inc_n=16)
        ms[("ld_yg3", i)] = (s_yg3, s_yg3.count)

        dorder = ([(k, s) for k in range(CH) for s in range(4)] if i == 0
                  else [(k, s) for s in range(4) for k in range(CH)])
        for k, s in dorder:
            w0 = ([(s_wdp[k // 6], 16)] if k % 6 == 0 else [])
            if k == 0:
                w0 = [ms[("ld_h3", i)], (s_v, M("v_evac3", i - 1, s))] + w0
                if i == 0 and s == 0:
                    w0 = first3["tensor"] + w0
            for half in range(2):
                A("tensor", lambda e, s=s, k=k, half=half, sl=sl: e.matmul(
                    pp[s].ap()[:, half * 512:(half + 1) * 512],
                    h3[sl].ap()[:, k * T + s * 128:k * T + (s + 1) * 128],
                    wdtp[k // 6].ap()[:, (k % 6) * 1024 + half * 512:(k % 6) * 1024 + (half + 1) * 512],
                    start=(k == 0), stop=False, skip_group_check=True),
                  waits=w0 if half == 0 else [],
                  inc=(s_t if (k == CH - 1 and half == 1) else None))
                w0 = []
            if k == CH - 1:
                ms[("t_down", i, s)] = s_t.count
        aorder = ([(k, s) for k in range(CE) for s in range(4)] if i == 0
                  else [(k, s) for s in range(4) for k in range(CE)])
        for k, s in aorder:
            w0 = ([ms[("ld_yg3", i)]] if k == 0 else []) \
                 + ([(s_wop[k // 8], 16)] if k % 8 == 0 else [])
            for half in range(2):
                A("tensor", lambda e, s=s, k=k, half=half: e.matmul(
                    pp[s].ap()[:, half * 512:(half + 1) * 512],
                    yg3.ap()[:, k * T + s * 128:k * T + (s + 1) * 128],
                    woutb.ap()[:, k * 1024 + half * 512:k * 1024 + (half + 1) * 512],
                    start=False, stop=(k == CE - 1), skip_group_check=True),
                  waits=w0 if half == 0 else [],
                  inc=(s_t if (k == CE - 1 and half == 1) else None))
                w0 = []
            if k == CE - 1:
                ms[("t_acc", i, s)] = s_t.count
        for s in range(4):
            A("vector", lambda e, s=s, sl=sl: e.tensor_add(
                outb.ap()[:, s * 1024:(s + 1) * 1024], pp[s].ap()[:, :],
                x3[sl].ap()[:, s * 1024:(s + 1) * 1024]),
              waits=first3["vector"] + [(s_t, ms[("t_acc", i, s)]),
                                        ms[("ld_x3", i)]]
                    + MP("st_out", i - 1, s),
              inc=s_v)
            first3["vector"] = []
            ms[("v_evac3", i, s)] = s_v.count
            A("sync", lambda e, i=i, s=s: e.dma_start(
                out_d[i * T + s * 128:i * T + (s + 1) * 128, :],
                outb.ap()[:, s * 1024:(s + 1) * 1024]),
              waits=first3["sync"] + [(s_v, ms[("v_evac3", i, s)])],
              inc=s_sout, inc_n=16)
            first3["sync"] = []
            ms[("st_out", i, s)] = (s_sout, s_sout.count)

    final_st = s_sout.count
    A("sync", lambda e: e.wait_ge(s_sout.h, final_st))

    with nc.Block() as block:
        block.tensor(_runner(sched, "tensor"))
        block.vector(_runner(sched, "vector"))
        block.scalar(_runner(sched, "scalar"))
        block.sync(_runner(sched, "sync"))
        block.gpsimd(_runner(sched, "gpsimd"))
    es.close()
    return nc


_NC_CACHE = {}


def kernel(L0_attn_x, L0_attn_Win, L0_attn_conv_w, L0_attn_conv_b, L0_attn_Wx,
           L0_attn_Wdt, L0_attn_bdt, L0_attn_A_log, L0_attn_D, L0_attn_ln_g,
           L0_attn_ln_b, L0_attn_Wout, L0_attn_res_scale, L0_ffn_w_gate,
           L0_ffn_w_up, L0_ffn_w_down, **_unused):
    x = np.asarray(L0_attn_x, dtype=np.float32)
    Win = np.ascontiguousarray(np.asarray(L0_attn_Win, np.float32)[:, :4096])
    Wout = np.ascontiguousarray(np.asarray(L0_attn_Wout, np.float32))
    wgT = np.ascontiguousarray(np.asarray(L0_ffn_w_gate).T).astype(BF)
    wuT = np.ascontiguousarray(np.asarray(L0_ffn_w_up).T).astype(BF)
    wdT = np.ascontiguousarray(np.asarray(L0_ffn_w_down).T).astype(BF)
    D = np.asarray(L0_attn_D, np.float32)
    cwv = np.asarray(L0_attn_conv_w, np.float32)
    cbv = np.asarray(L0_attn_conv_b, np.float32)
    lngv = np.asarray(L0_attn_ln_g, np.float32)
    lnbv = np.asarray(L0_attn_ln_b, np.float32)
    rsv = np.asarray(L0_attn_res_scale, np.float32)

    dcol = np.ascontiguousarray(D.reshape(CE, 128).T)
    cw = np.ascontiguousarray(cwv.reshape(CE, 128, 4).transpose(1, 0, 2).reshape(128, CE * 4))
    cb = np.ascontiguousarray(cbv.reshape(CE, 128).T)
    lng = np.ascontiguousarray(lngv.reshape(CE, 128).T)
    lnb = np.ascontiguousarray(lnbv.reshape(CE, 128).T)
    rs = np.full((128, 1), rsv[0], np.float32)

    in_maps = []
    for c in range(8):
        b, hh = c // 2, c % 2
        s0 = hh * NTOK
        xseq = x[b]
        xTh = np.zeros((1024, HALO + NTOK), np.float32)
        xTh[:, HALO:] = xseq[s0:s0 + NTOK].T
        if s0 > 0:
            xTh[:, HALO - 3:HALO] = xseq[s0 - 3:s0].T
        in_maps.append({
            "xTh": np.ascontiguousarray(xTh),
            "xnat": np.ascontiguousarray(xseq[s0:s0 + NTOK]),
            "win": Win, "wout": Wout, "wgt": wgT, "wut": wuT, "wdt": wdT,
            "dcol": dcol, "cw": cw, "cb": cb, "lng": lng, "lnb": lnb, "rs": rs,
        })

    nc = build_nc()
    res = run_bass_kernel_spmd(nc, in_maps, core_ids=list(range(8)))
    out = np.empty((4, 8192, 1024), np.float32)
    for c in range(8):
        b, hh = c // 2, c % 2
        out[b, hh * NTOK:(hh + 1) * NTOK] = res.results[c]["out"]
    return out


if __name__ == "__main__":
    rng = np.random.RandomState(0)
    print("smoke build only")
    build_nc()
    print("build ok")


# version 31
# speedup vs baseline: 1.2252x; 1.2252x over previous
import numpy as np
import ml_dtypes
import concourse.bass as bass
import concourse.mybir as mybir
from concourse.bass_utils import run_bass_kernel_spmd
from contextlib import ExitStack

BF16 = mybir.dt.bfloat16
F32 = mybir.dt.float32
AF = mybir.ActivationFunctionType
OP = mybir.AluOpType
BF = ml_dtypes.bfloat16

NT = 8
T = 512
KD = 8
CE = 16
CH = 24
MO = 8
NTOK = NT * T
HALO = 32
S127 = float(np.float32(1.0 / 127.0))
S127SQ = float(np.float32(1.0 / 127.0) * np.float32(1.0 / 127.0))
EPS = 1e-5

ENGINES = ("tensor", "vector", "scalar", "sync", "gpsimd")


class Sem:
    def __init__(self, handle):
        self.h = handle
        self.count = 0


class Sched:
    def __init__(self):
        self.ops = {e: [] for e in ENGINES}
        self.autosync = {}

    def add(self, eng, fn, waits=(), inc=None, inc_n=1):
        w = [(s, v) for (s, v) in waits if v > 0]
        aus = self.autosync.get(eng)
        if aus is not None:
            if aus.count > 0:
                w = w + [(aus, aus.count)]
            if inc is None:
                inc = aus
        incs = []
        val = None
        if inc is not None:
            inc.count += inc_n
            val = inc.count
            incs.append((inc, inc_n))
        self.ops[eng].append((w, fn, incs))
        return val


def _runner(sched, name):
    def go(eng):
        lastw = {}
        for waits, fn, incs in sched.ops[name]:
            for sem, val in waits:
                k = id(sem)
                if val > lastw.get(k, 0):
                    eng.wait_ge(sem.h, val)
                    lastw[k] = val
            inst = fn(eng)
            for sem, n in incs:
                inst.then_inc(sem.h, n)
    return go


def build_nc(debug=False):
    nc = bass.Bass("TRN2", target_bir_lowering=False)

    xTh = nc.dram_tensor("xTh", [1024, HALO + NTOK], F32, kind="ExternalInput").ap()
    xnat = nc.dram_tensor("xnat", [NTOK, 1024], F32, kind="ExternalInput").ap()
    win = nc.dram_tensor("win", [1024, 4096], F32, kind="ExternalInput").ap()
    wout = nc.dram_tensor("wout", [2048, 1024], F32, kind="ExternalInput").ap()
    wgt_d = nc.dram_tensor("wgt", [1024, 3072], BF16, kind="ExternalInput").ap()
    wut_d = nc.dram_tensor("wut", [1024, 3072], BF16, kind="ExternalInput").ap()
    wdt_d = nc.dram_tensor("wdt", [3072, 1024], BF16, kind="ExternalInput").ap()
    dcol_d = nc.dram_tensor("dcol", [128, CE], F32, kind="ExternalInput").ap()
    cw_d = nc.dram_tensor("cw", [128, CE * 4], F32, kind="ExternalInput").ap()
    cb_d = nc.dram_tensor("cb", [128, CE], F32, kind="ExternalInput").ap()
    lng_d = nc.dram_tensor("lng", [128, CE], F32, kind="ExternalInput").ap()
    lnb_d = nc.dram_tensor("lnb", [128, CE], F32, kind="ExternalInput").ap()
    rs_d = nc.dram_tensor("rs", [128, 1], F32, kind="ExternalInput").ap()
    out_d = nc.dram_tensor("out", [NTOK, 1024], F32, kind="ExternalOutput").ap()

    scr_kind = "ExternalOutput" if debug else "Internal"
    scr_xn = nc.dram_tensor("scr_xn", [NT, 128, KD * T], BF16, kind=scr_kind).ap()
    scr_yg = nc.dram_tensor("scr_yg", [NT, 128, CE * T], BF16, kind=scr_kind).ap()
    scr_h = nc.dram_tensor("scr_h", [NT, 128, CH * T], BF16, kind=scr_kind).ap()

    avgb = nc.alloc_sbuf_tensor("avgb", [128, 128], BF16)
    Db = nc.alloc_sbuf_tensor("Db", [128, CE * 128], BF16)
    D2b = nc.alloc_sbuf_tensor("D2b", [128, CE * 128], BF16)
    dcol = nc.alloc_sbuf_tensor("dcolS", [128, CE], F32)
    cw = nc.alloc_sbuf_tensor("cwS", [128, CE * 4], F32)
    cb = nc.alloc_sbuf_tensor("cbS", [128, CE], F32)
    lng = nc.alloc_sbuf_tensor("lngS", [128, CE], F32)
    lnb = nc.alloc_sbuf_tensor("lnbS", [128, CE], F32)
    gq = nc.alloc_sbuf_tensor("gq", [128, CE], F32)
    bq = nc.alloc_sbuf_tensor("bq", [128, CE], F32)
    rs = nc.alloc_sbuf_tensor("rsS", [128, 1], F32)
    epsc = nc.alloc_sbuf_tensor("epsc", [128, 1], F32)
    onec = nc.alloc_sbuf_tensor("onec", [128, 1], F32)
    s2c = nc.alloc_sbuf_tensor("s2c", [128, 1], F32)
    ARENA = 202816
    arena_buf = nc.alloc_sbuf_tensor("arena", [128, ARENA], mybir.dt.uint8)
    base = nc.lookup_mloc(arena_buf).addr

    def at(name, shape, dtype, off):
        nbytes = int(np.prod(shape[1:])) * (2 if dtype == BF16 else 4)
        assert off + nbytes <= ARENA, (name, off, nbytes)
        return nc.alloc_sbuf_tensor_at(name, list(shape), dtype, offset=base + off)

    xb = [at("xb0", [128, KD * T], BF16, 0),
          at("xb1", [128, KD * T], BF16, 8192)]
    x2f = at("x2f", [128, KD * T], BF16, 16384)
    xp = at("xp", [128, CE * (T + HALO)], BF16, 24576)
    xs2r = at("xs2r", [128, 2 * T], BF16, 41984)
    diag = at("diag", [128, CE * 4 * 128], BF16, 44032)
    mu1 = at("mu1", [128, T], BF16, 60416)
    r1 = at("r1", [128, T], BF16, 61440)
    tmf = at("tmf", [128, T], BF16, 62464)
    v1t = at("v1t", [128, T], F32, 63488)
    xn = [at("xn0", [128, KD * T], BF16, 65536),
          at("xn1", [128, KD * T], BF16, 73728)]
    winb = at("winb", [128, KD * 4096], BF16, 81920)
    xs = at("xs", [128, CE * T], BF16, 147456)
    sigz = at("sigz", [128, CE * T], BF16, 163840)
    yg = at("yg", [128, CE * T], BF16, 180224)
    tA = at("tA", [128, T], BF16, 196608)
    tB = at("tB", [128, T], BF16, 197632)
    mu2 = at("mu2", [128, T], BF16, 198656)
    r2 = at("r2", [128, T], BF16, 199680)
    sS1 = at("sS1", [128, T], F32, 200704)
    o = 180224
    def at2(name, shape, dtype):
        nonlocal o
        t = at(name, shape, dtype, o)
        o += int(np.prod(shape[1:])) * (2 if dtype == BF16 else 4)
        return t
    xbh = at2("xbh", [128, KD * HALO], BF16)
    xbh2 = at2("xbh2", [128, KD * HALO], BF16)
    xnh = at2("xnh", [128, KD * HALO], BF16)
    muh = at2("muh", [128, HALO], F32)
    varh = at2("varh", [128, HALO], F32)
    rh = at2("rh", [128, HALO], F32)
    sSh = at2("sSh", [128, HALO], F32)
    tmh = at2("tmh", [128, HALO], F32)
    identb = at2("identb", [128, 128], BF16)
    itmp = at2("itmp", [128, 128], BF16)
    wgt = at("wgtS", [128, KD * 3072], BF16, 0)
    xn2 = [at("xn2_0", [128, KD * T], BF16, 49152),
           at("xn2_1", [128, KD * T], BF16, 57344)]
    wut = at("wutS", [128, KD * 3072], BF16, 65536)
    hbuf = at("hbuf", [128, CH * T], BF16, 116736)
    woutb = at("woutb", [128, CE * 1024], BF16, 0)
    wdt3 = at("wdt3", [128, 6 * 1024], BF16, 32768)
    yg3 = at("yg3", [128, CE * T], BF16, 45056)
    h3 = [at("h3_0", [128, CH * T], BF16, 141312),
          at("h3_1", [128, CH * T], BF16, 61440)]
    x3 = [at("x3_0", [128, 4 * 1024], F32, 86016),
          at("x3_1", [128, 4 * 1024], F32, 102400)]
    outb = at("outb", [128, 4 * 1024], F32, 118784)
    wdtp = [at(f"wdtp{j}", [128, 6 * 1024], BF16, 165888 + 12288 * j)
            for j in range(3)] + [wdt3]

    pp = [nc.alloc_psum_tensor(f"pp{j}", [128, 1024], F32) for j in range(4)]

    sched = Sched()
    es = ExitStack()

    def mksem(nm):
        return Sem(es.enter_context(nc.semaphore(nm)))
    s_t = mksem("s_t")
    s_v = mksem("s_v")
    s_s = mksem("s_s")
    s_g = mksem("s_g")
    s_cst = mksem("s_cst")
    s_halo = mksem("s_halo")
    s_winp = [mksem(f"s_winp{j}") for j in range(16)]
    s_xb = [mksem("s_xb0"), mksem("s_xb1")]
    s_xn2 = [mksem("s_xn2_0"), mksem("s_xn2_1")]
    s_yg3 = mksem("s_yg3")
    s_h3 = [mksem("s_h3_0"), mksem("s_h3_1")]
    s_x3 = [mksem("s_x3_0"), mksem("s_x3_1")]
    s_sxn = [mksem("s_sxn0"), mksem("s_sxn1")]
    s_syg = mksem("s_syg")
    s_shA = mksem("s_shA")
    s_shB = mksem("s_shB")
    s_sout = mksem("s_sout")
    s_dbg = mksem("s_dbg")
    s_wgp = [mksem(f"s_wgp{j}") for j in range(8)]
    s_wup = [mksem(f"s_wup{j}") for j in range(8)]
    s_wdp = [mksem(f"s_wdp{j}") for j in range(4)]
    s_wop = [mksem(f"s_wop{j}") for j in range(2)]
    sched.autosync = {"vector": s_v}

    A = sched.add

    for tile, valv in ((avgb, 1.0 / 1024.0), (epsc, EPS), (onec, 1.0), (s2c, S127SQ)):
        A("gpsimd", lambda e, t=tile, v=valv: e.memset(t.ap()[:, :], v), inc=s_g)
    pro_g = s_g.count
    A("gpsimd", lambda e: e.dma_start(
        xbh.ap().rearrange("p (k t) -> p k t", t=HALO),
        xTh[:, 0:HALO].rearrange("(k p) t -> p k t", p=128)), inc=s_halo, inc_n=16)
    A("gpsimd", lambda e: e.dma_start(
        xb[0].ap().rearrange("p (k t) -> p k t", k=KD),
        xTh[:, HALO: HALO + T].rearrange("(k p) t -> p k t", p=128)),
      inc=s_xb[0], inc_n=16)
    ms0_ld_xb0 = (s_xb[0], s_xb[0].count)
    A("gpsimd", lambda e: e.iota(itmp.ap()[:, :], [[1, 128]], channel_multiplier=-1,
                                 allow_small_or_imprecise_dtypes=True), inc=s_g)
    pro_g2 = s_g.count
    for name, dst, src in (
        ("dcol", dcol, dcol_d), ("cw", cw, cw_d), ("cb", cb, cb_d),
        ("lng", lng, lng_d), ("lnb", lnb, lnb_d), ("rs", rs, rs_d),
    ):
        A("sync", lambda e, d=dst, s=src: e.dma_start(d.ap()[:, :], s),
          inc=s_cst, inc_n=16)
    ld_cst = s_cst.count

    def emit_win_part(j):
        A("gpsimd", lambda e, j=j: e.dma_start(
            winb.ap().rearrange("p (k n) -> p k n", k=KD)[:, :, j * 512:(j + 1) * 512],
            win[:, j * 512:(j + 1) * 512].rearrange("(k p) n -> p k n", p=128)),
          inc=s_winp[j], inc_n=16)
    emit_win_part(0)
    A("gpsimd", lambda e: e.dma_start(
        xb[1].ap().rearrange("p (k t) -> p k t", k=KD),
        xTh[:, HALO + T: HALO + 2 * T].rearrange("(k p) t -> p k t", p=128)),
      inc=s_xb[1], inc_n=16)
    ms0_ld_xb1 = (s_xb[1], s_xb[1].count)
    for j in range(1, 8):
        emit_win_part(j)

    def emit_const_derivs():
        A("vector", lambda e: e.tensor_single_scalar(identb.ap()[:, :], itmp.ap()[:, :],
                                                     0.0, OP.is_equal),
          waits=[(s_g, pro_g2), (s_cst, ld_cst)])
        pdc, dbp = [], []
        for c in range(CE):
            for k in range(4):
                A("vector", lambda e, c=c, k=k: e.tensor_scalar_mul(
                    diag.ap()[:, (c * 4 + k) * 128:(c * 4 + k + 1) * 128],
                    identb.ap()[:, :], cw.ap()[:, c * 4 + k:c * 4 + k + 1]))
            pdc.append(s_v.count)
            A("vector", lambda e, c=c: e.tensor_scalar(
                Db.ap()[:, c * 128:(c + 1) * 128], avgb.ap()[:, :],
                dcol.ap()[:, c:c + 1], 0.5, OP.mult, OP.mult),
              waits=[(s_g, pro_g)], inc=s_v)
            A("vector", lambda e, c=c: e.tensor_scalar_mul(
                D2b.ap()[:, c * 128:(c + 1) * 128],
                Db.ap()[:, c * 128:(c + 1) * 128], dcol.ap()[:, c:c + 1]), inc=s_v)
            dbp.append(s_v.count)
        A("vector", lambda e: e.tensor_scalar_mul(gq.ap()[:, :], lng.ap()[:, :], rs.ap()[:, :1]), inc=s_v)
        A("vector", lambda e: e.tensor_scalar_mul(bq.ap()[:, :], lnb.ap()[:, :], rs.ap()[:, :1]), inc=s_v)
        return pdc, dbp, s_v.count

    def ln_stats(engv, psum_pair, mu_sl, r_sl, n, wait_t_val, pre_v=()):
        tiny = True
        A("vector", lambda e: e.tensor_copy(mu_sl[:, 0:n], psum_pair.ap()[:, 0:n]),
          waits=[(s_t, wait_t_val)] + [w for w in pre_v if w[1] > 0], inc=s_v)
        cpv = s_v.count
        A("vector", lambda e: e.tensor_mul(tmf.ap()[:, 0:n] if n == T else tmh.ap()[:, 0:n],
                                           mu_sl[:, 0:n], mu_sl[:, 0:n]),
          waits=([(s_v, cpv)] if tiny else []), inc=s_v)
        mlv = s_v.count
        tm = tmf.ap() if n == T else tmh.ap()
        vt = v1t.ap() if n == T else varh.ap()
        sS = sS1.ap() if n == T else sSh.ap()
        A("vector", lambda e: e.tensor_sub(vt[:, 0:n], psum_pair.ap()[:, 512:512 + n], tm[:, 0:n]),
          waits=([(s_v, mlv)] if tiny else []), inc=s_v)
        v_done = s_v.count
        A("scalar", lambda e: e.activation(sS[:, 0:n], vt[:, 0:n], AF.Ln,
                                           bias=epsc.ap()[:, :1], scale=onec.ap()[:, :1]),
          waits=[(s_v, v_done), (s_g, pro_g)], inc=s_s)
        lnv = s_s.count
        A("scalar", lambda e: e.activation(r_sl[:, 0:n], sS[:, 0:n], AF.Exp, scale=-0.5),
          waits=([(s_s, lnv)] if tiny else []), inc=s_s)
        return v_done, s_s.count

    n = HALO
    A("vector", lambda e: e.tensor_mul(xbh2.ap()[:, :], xbh.ap()[:, :], xbh.ap()[:, :]),
      waits=[(s_halo, 16)], inc=s_v)
    v_x2h = s_v.count
    for k in range(KD):
        A("tensor", lambda e, k=k: e.matmul(
            pp[2].ap()[:, 0:n], avgb.ap()[:, :], xbh.ap()[:, k * n:(k + 1) * n],
            start=(k == 0), stop=(k == KD - 1)),
          waits=[(s_halo, 16), (s_g, pro_g)])
        A("tensor", lambda e, k=k: e.matmul(
            pp[2].ap()[:, 512:512 + n], avgb.ap()[:, :], xbh2.ap()[:, k * n:(k + 1) * n],
            start=(k == 0), stop=(k == KD - 1)),
          waits=[(s_v, v_x2h)], inc=(s_t if k == KD - 1 else None))
    t_stats_h = s_t.count
    v_done_h, s_rh = ln_stats("v", pp[2], muh.ap(), rh.ap(), HALO, t_stats_h)
    for k in range(KD):
        A("vector", lambda e, k=k: e.tensor_sub(
            tA.ap()[:, 0:n], xbh.ap()[:, k * n:(k + 1) * n], muh.ap()[:, 0:n]),
          waits=[(s_s, s_rh)], inc=s_v)
        sbv = s_v.count
        A("vector", lambda e, k=k: e.tensor_mul(
            xnh.ap()[:, k * n:(k + 1) * n], tA.ap()[:, 0:n], rh.ap()[:, 0:n]),
          waits=[(s_v, sbv)], inc=s_v)
    v_xnh = s_v.count
    ev_h = []
    for m in range(CE):
        for k in range(KD):
            A("tensor", lambda e, m=m, k=k: e.matmul(
                pp[3].ap()[:, (m % 2) * 512:(m % 2) * 512 + n],
                winb.ap()[:, k * 4096 + m * 128:k * 4096 + (m + 1) * 128],
                xnh.ap()[:, k * n:(k + 1) * n],
                start=(k == 0), stop=(k == KD - 1)),
              waits=([(s_v, v_xnh), (s_winp[m // 4], 16)] if k == 0 else [])
                    + ([(s_s, ev_h[m - 2])] if (k == 0 and m >= 2) else []),
              inc=(s_t if k == KD - 1 else None))
        t_m = s_t.count
        A("scalar", lambda e, m=m: e.copy(
            xp.ap()[:, m * (T + HALO):m * (T + HALO) + HALO],
            pp[3].ap()[:, (m % 2) * 512:(m % 2) * 512 + n]),
          waits=[(s_t, t_m)], inc=s_s)
        ev_h.append(s_s.count)

    if debug:
        for nm, buf, wl in (
            ("dbg_xbh", xbh, [(s_halo, 16)]),
            ("dbg_xbh2", xbh2, [(s_v, v_x2h)]),
            ("dbg_muh", muh, [(s_s, s_rh)]),
            ("dbg_varh", varh, [(s_s, s_rh)]),
            ("dbg_rh", rh, [(s_s, s_rh)]),
            ("dbg_xnh", xnh, [(s_v, v_xnh)]),
        ):
            shp = [buf.shape[0], buf.shape[1]]
            dt_ = nc.dram_tensor(nm, shp, buf.dtype, kind="ExternalOutput").ap()
            A("sync", lambda e, d=dt_, b=buf: e.dma_start(d, b.ap()[:, :]),
              waits=wl, inc=s_dbg, inc_n=16)

    ms = {}
    ms[("ld_xb", 0)] = ms0_ld_xb0
    ms[("ld_xb", 1)] = ms0_ld_xb1

    def M(name, i, *rest):
        return ms.get((name, i) + rest, 0)

    def MP(name, i, *rest):
        p = ms.get((name, i) + rest)
        return [p] if p else []

    def emit_ld_xb(j):
        if j >= NT:
            return
        if ("ld_xb", j) in ms:
            return
        A("gpsimd", lambda e, j=j: e.dma_start(
            xb[j % 2].ap().rearrange("p (k t) -> p k t", k=KD),
            xTh[:, HALO + j * T: HALO + (j + 1) * T].rearrange("(k p) t -> p k t", p=128)),
          waits=[(s_v, M("v_xn", j - 2))], inc=s_xb[j % 2], inc_n=16)
        ms[("ld_xb", j)] = (s_xb[j % 2], s_xb[j % 2].count)

    def emit_x2(j, ks=None):
        if j >= NT:
            return
        for k in (range(KD) if ks is None else ks):
            A("gpsimd", lambda e, k=k, j=j: e.tensor_mul(
                x2f.ap()[:, k * T:(k + 1) * T],
                xb[j % 2].ap()[:, k * T:(k + 1) * T], xb[j % 2].ap()[:, k * T:(k + 1) * T]),
              waits=[ms[("ld_xb", j)], (s_t, M("t_s1b", j - 1, k))], inc=s_g)
            ms[("g_x2", j, k)] = s_g.count

    def emit_s1k(j, k):
        if j >= NT:
            return
        A("tensor", lambda e, k=k, j=j: e.matmul(
            pp[0].ap()[:, 0:T], avgb.ap()[:, :], xb[j % 2].ap()[:, k * T:(k + 1) * T],
            start=(k == 0), stop=(k == KD - 1)),
          waits=([(s_s, M("s_r1", j - 1)), ms[("ld_xb", j)]] if k == 0 else []))
        A("tensor", lambda e, k=k: e.matmul(
            pp[0].ap()[:, 512:512 + T], avgb.ap()[:, :],
            x2f.ap()[:, k * T:(k + 1) * T],
            start=(k == 0), stop=(k == KD - 1)),
          waits=[(s_g, ms[("g_x2", j, k)])], inc=s_t)
        ms[("t_s1b", j, k)] = s_t.count
        if k == KD - 1:
            ms[("t_stats1", j)] = s_t.count

    def emit_stats1(j):
        for k in range(KD):
            emit_s1k(j, k)

    def vp1_vec(j):
        if j >= NT:
            return
        A("vector", lambda e: e.tensor_copy(mu1.ap()[:, :], pp[0].ap()[:, 0:T]),
          waits=[(s_t, ms[("t_stats1", j)])], inc=s_v)
        A("vector", lambda e: e.tensor_mul(tmf.ap()[:, :], mu1.ap()[:, :], mu1.ap()[:, :]), inc=s_v)
        A("vector", lambda e: e.tensor_sub(pp[0].ap()[:, 0:T], pp[0].ap()[:, 512:1024],
                                           tmf.ap()[:, :]), inc=s_v)
        ms[("v_var1", j)] = s_v.count

    def vp1_scl(j):
        if j >= NT:
            return
        A("scalar", lambda e: e.activation(sS1.ap()[:, :], pp[0].ap()[:, 0:T], AF.Ln,
                                           bias=epsc.ap()[:, :1], scale=onec.ap()[:, :1]),
          waits=[(s_v, ms[("v_var1", j)]), (s_g, pro_g)], inc=s_s)
        lnv = s_s.count
        A("scalar", lambda e: e.activation(r1.ap()[:, :], sS1.ap()[:, :], AF.Exp, scale=-0.5),
          waits=[(s_s, lnv), (s_v, M("v_xn", j - 1))], inc=s_s)
        ms[("s_r1", j)] = s_s.count

    def vp2_vec(j):
        A("vector", lambda e: e.tensor_copy(mu2.ap()[:, :], pp[2].ap()[:, 0:T]),
          waits=[(s_t, ms[("t_stats2", j)])], inc=s_v)
        A("vector", lambda e: e.tensor_mul(tmf.ap()[:, :], mu2.ap()[:, :], mu2.ap()[:, :]), inc=s_v)
        A("vector", lambda e: e.tensor_sub(pp[2].ap()[:, 0:T], pp[2].ap()[:, 512:1024],
                                           tmf.ap()[:, :]), inc=s_v)
        ms[("v_var2", j)] = s_v.count

    def vp2_scl(j):
        A("scalar", lambda e: e.activation(sS1.ap()[:, :], pp[2].ap()[:, 0:T], AF.Ln,
                                           bias=epsc.ap()[:, :1], scale=onec.ap()[:, :1]),
          waits=[(s_v, ms[("v_var2", j)]), (s_g, pro_g)], inc=s_s)
        ms[("s_lnv2", j)] = s_s.count
        lnv = s_s.count
        A("scalar", lambda e: e.activation(r2.ap()[:, :], sS1.ap()[:, :], AF.Exp, scale=-0.5),
          waits=[(s_s, lnv), (s_v, M("v_chainc", j - 1, CE - 1))], inc=s_s)
        ms[("s_r2", j)] = s_s.count

    def emit_xn_part(j, part):
        if j >= NT:
            return
        for k in range(2 * part, 2 * part + 2):
            A("vector", lambda e, k=k, j=j: e.tensor_sub(
                tA.ap()[:, :], xb[j % 2].ap()[:, k * T:(k + 1) * T], mu1.ap()[:, :]),
              waits=[(s_s, ms[("s_r1", j)])]
                    + ((MP("st_xn", j - 2) + [(s_t, M("t_pz", j - 2, 2 * CE - 1))])
                       if k == 2 * part else []))
            A("vector", lambda e, k=k, j=j: e.tensor_mul(
                xn[j % 2].ap()[:, k * T:(k + 1) * T], tA.ap()[:, :], r1.ap()[:, :]))
        if part == 3:
            ms[("v_xn", j)] = s_v.count
            A("sync", lambda e, j=j: e.dma_start(scr_xn[j], xn[j % 2].ap()[:, :]),
              waits=[(s_v, ms[("v_xn", j)])], inc=s_sxn[j % 2], inc_n=16)
            ms[("st_xn", j)] = (s_sxn[j % 2], s_sxn[j % 2].count)

    def emit_chain_c(j, c):
        A("vector", lambda e, c=c: e.scalar_tensor_tensor(
            tA.ap()[:, :], xs.ap()[:, c * T:(c + 1) * T], dcol.ap()[:, c:c + 1],
            mu2.ap()[:, :], OP.mult, OP.subtract),
          waits=[(s_s, ms[("s_silu", j, c)])])
        ms[("v_ch1", j, c)] = s_v.count
        A("vector", lambda e, c=c: e.scalar_tensor_tensor(
            tB.ap()[:, :], tA.ap()[:, :], gq.ap()[:, c:c + 1],
            r2.ap()[:, :], OP.mult, OP.mult),
          waits=([(s_s, ms[("s_r2", j)])] if c == 0 else []))
        A("vector", lambda e, c=c: e.scalar_tensor_tensor(
            yg.ap()[:, c * T:(c + 1) * T], tB.ap()[:, :], bq.ap()[:, c:c + 1],
            sigz.ap()[:, c * T:(c + 1) * T], OP.add, OP.mult),
          waits=(MP("st_yg", j - 1) if c == 0 else []))
        ms[("v_chainc", j, c)] = s_v.count
        if c == CE - 1:
            ms[("v_chain", j)] = s_v.count
            A("sync", lambda e, j=j: e.dma_start(scr_yg[j], yg.ap()[:, :]),
              waits=[(s_v, ms[("v_chain", j)])], inc=s_syg, inc_n=16)
            ms[("st_yg", j)] = (s_syg, s_syg.count)

    emit_x2(0)
    emit_stats1(0)
    vp1_vec(0)
    vp1_scl(0)
    for p_ in range(4):
        emit_xn_part(0, p_)
    emit_x2(1)
    emit_stats1(1)
    vp1_vec(1)
    pro_diag_c, db_pair, pro_v = emit_const_derivs()
    pro_diag = pro_v

    for i in range(NT):
        sl = i % 2
        emit_ld_xb(i + 2)

        if i > 0:
            vp2_scl(i - 1)
        vp1_scl(i + 1)

        if i > 0:
            A("vector", lambda e: e.tensor_copy(
                xp.ap().rearrange("p (c w) -> p c w", c=CE)[:, :, HALO - 8:HALO],
                xp.ap().rearrange("p (c w) -> p c w", c=CE)[:, :, HALO + T - 8:HALO + T]),
              waits=[(s_s, M("s_evac", i - 1, 15)), (s_t, M("t_conv", i - 1, CE - 1))])
            ms[("v_halo", i)] = s_v.count

        def emit_sigz(c):
            src_pp = pp[1] if c % 4 < 2 else pp[3]
            half = (c % 4) % 2
            A("scalar", lambda e, c=c, src_pp=src_pp, half=half: e.activation(
                sigz.ap()[:, c * T:(c + 1) * T],
                src_pp.ap()[:, half * 512:(half + 1) * 512], AF.Sigmoid),
              waits=[(s_t, ms[("t_pz", i, CE + c)]),
                     (s_v, M("v_chainc", i - 1, c))], inc=s_s)
            ms[("s_sigz", i, c)] = s_s.count

        def emit_conv(c):
            base = c * (T + HALO) + (HALO - 3)
            w0 = [(s_s, ms[("s_evac", i, c)]),
                  (s_s, (M("s_silu", i, c - 2) if c >= 2 else
                         (M("s_sigz", i - 1, 14 + c) if i > 0 else ev_h[14 + c]))),
                  (s_v, pro_diag_c[c] if i == 0 else pro_diag)]
            if i > 0:
                w0 = w0 + [(s_v, M("v_halo", i))]
            for k in range(4):
                A("tensor", lambda e, c=c, k=k, base=base: e.matmul(
                    pp[3].ap()[:, (c % 2) * 512:(c % 2 + 1) * 512],
                    diag.ap()[:, (c * 4 + k) * 128:(c * 4 + k + 1) * 128],
                    xp.ap()[:, base + k:base + k + T],
                    start=(k == 0), stop=(k == 3)),
                  waits=w0 if k == 0 else [],
                  inc=(s_t if k == 3 else None))
            ms[("t_conv", i, c)] = s_t.count

        def emit_silu(c):
            A("scalar", lambda e, c=c: e.activation(
                xs.ap()[:, c * T:(c + 1) * T],
                pp[3].ap()[:, (c % 2) * 512:(c % 2 + 1) * 512],
                AF.Silu, bias=cb.ap()[:, c:c + 1], scale=onec.ap()[:, :1]),
              waits=[(s_t, ms[("t_conv", i, c)]),
                     (s_v, M("v_ch1", i - 1, c))], inc=s_s)
            ms[("s_silu", i, c)] = s_s.count

        def emit_xs2(c):
            A("gpsimd", lambda e, c=c: e.tensor_mul(
                xs2r.ap()[:, (c % 2) * T:(c % 2 + 1) * T],
                xs.ap()[:, c * T:(c + 1) * T], xs.ap()[:, c * T:(c + 1) * T]),
              waits=[(s_s, ms[("s_silu", i, c)]), (s_t, M("t_s2b", i, c - 2))],
              inc=s_g)
            ms[("g_xs2", i, c)] = s_g.count

        def emit_sv(c):
            A("tensor", lambda e, c=c: e.matmul(
                pp[2].ap()[:, 0:T], Db.ap()[:, c * 128:(c + 1) * 128],
                xs.ap()[:, c * T:(c + 1) * T], start=(c == 0), stop=(c == CE - 1)),
              waits=[(s_s, ms[("s_silu", i, c)]),
                     (s_v, db_pair[c] if i == 0 else 0)]
                    + (([(s_s, M("s_lnv2", i - 1))] if i > 0 else [(s_v, v_done_h)])
                       if c == 0 else []))
            A("tensor", lambda e, c=c: e.matmul(
                pp[2].ap()[:, 512:512 + T], D2b.ap()[:, c * 128:(c + 1) * 128],
                xs2r.ap()[:, (c % 2) * T:(c % 2 + 1) * T],
                start=(c == 0), stop=(c == CE - 1)),
              waits=[(s_g, ms[("g_xs2", i, c)])], inc=s_t)
            ms[("t_s2b", i, c)] = s_t.count
            if c == CE - 1:
                ms[("t_stats2", i)] = s_t.count

        def zbank(c):
            return (pp[1] if c % 4 < 2 else pp[3]), (c % 4) % 2
        for m in range(2 * CE):
            if 0 <= m - 18 < CE - 2:
                emit_sigz(m - 18)
            first_w = [(s_v, ms[("v_xn", i)])] if m == 0 else []
            if m < 2:
                bank_w = [(s_s, M("s_sigz", i - 1, 12 + m))] if i > 0 else []
            elif m < CE + 2:
                bank_w = [(s_s, M("s_evac", i, m - 2))]
            else:
                c = m - 16
                if c < 4:
                    bank_w = [(s_s, ms[("s_silu", i, 12 + c)])]
                else:
                    bank_w = [(s_s, M("s_sigz", i, c - 4))]
            if m < CE:
                dst, half = pp[1], m % 2
            else:
                dst, half = zbank(m - 16)
            for k in range(KD):
                A("tensor", lambda e, m=m, k=k, sl=sl, dst=dst, half=half: e.matmul(
                    dst.ap()[:, half * 512:(half + 1) * 512],
                    winb.ap()[:, k * 4096 + m * 128:k * 4096 + (m + 1) * 128],
                    xn[sl].ap()[:, k * T:(k + 1) * T],
                    start=(k == 0), stop=(k == KD - 1)),
                  waits=(first_w + bank_w
                         + [(s_winp[m // 4], 16)]) if k == 0 else [],
                  inc=(s_t if k == KD - 1 else None))
            ms[("t_pz", i, m)] = s_t.count
            if m < CE:
                hw = [(s_v, M("v_halo", i))] if i > 0 else []
                A("scalar", lambda e, m=m: e.copy(
                    xp.ap()[:, m * (T + HALO) + HALO:(m + 1) * (T + HALO)],
                    pp[1].ap()[:, (m % 2) * 512:(m % 2 + 1) * 512]),
                  waits=[(s_t, ms[("t_pz", i, m)]), (s_t, M("t_conv", i - 1, m))] + hw,
                  inc=s_s)
                ms[("s_evac", i, m)] = s_s.count
            if 0 <= m - 2 < CE:
                emit_conv(m - 2)
                emit_silu(m - 2)
            if 0 <= m - 3 < CE:
                emit_xs2(m - 3)
            if 10 <= m <= 17:
                emit_x2(i + 2, [m - 10])
            if i > 0 and m < CE:
                emit_chain_c(i - 1, m)
            if 0 <= m - 4 < CE:
                emit_sv(m - 4)
            if 17 <= m <= 20:
                emit_xn_part(i + 1, m - 17)
            if 0 <= m - 12 < KD:
                emit_s1k(i + 2, m - 12)
            if m == 21:
                vp2_vec(i)
            if m == 23:
                vp1_vec(i + 2)
        emit_sigz(14)
        emit_sigz(15)
        if debug and i == 0:
            for nm, buf, wl in (
                ("dbg_xn", xn[0], [(s_v, ms[("v_xn", 0)])]),
                ("dbg_xp", xp, [(s_s, ms[("s_evac", 0, CE - 1)])]),
                ("dbg_sigz", sigz, [(s_s, ms[("s_sigz", 0, CE - 1)])]),
                ("dbg_xs", xs, [(s_s, ms[("s_silu", 0, CE - 1)])]),
                ("dbg_yg", yg, [(s_v, ms[("v_chain", 0)])]),
            ):
                shp = [buf.shape[0], buf.shape[1]]
                dt_ = nc.dram_tensor(nm, shp, buf.dtype, kind="ExternalOutput").ap()
                A("sync", lambda e, d=dt_, b=buf: e.dma_start(d, b.ap()[:, :]),
                  waits=wl, inc=s_dbg, inc_n=16)

    p1_pz_done = ms[("t_pz", NT - 1, 2 * CE - 1)]
    p1_conv_done = ms[("t_conv", NT - 1, CE - 1)]
    p1_s2b_done = ms[("t_s2b", NT - 1, CE - 1)]
    p1_evac_done = ms[("s_evac", NT - 1, CE - 1)]
    p1_sigz_done = ms[("s_sigz", NT - 1, CE - 1)]
    p1_var2_done = ms[("v_var2", NT - 1)]
    NP2 = 384
    E1 = [(s_t, p1_s2b_done), (s_s, p1_evac_done)]
    E2 = [(s_t, p1_conv_done), (s_v, p1_var2_done)]
    A("sync", lambda e: e.dma_start(xn2[0].ap()[:, :], scr_xn[0]),
      waits=E2 + [(s_sxn[0], 16)], inc=s_xn2[0], inc_n=16)
    ms[("ld_xn2", 0)] = (s_xn2[0], s_xn2[0].count)
    for j in range(8):
        A("gpsimd", lambda e, j=j: e.dma_start(
            wgt.ap().rearrange("p (k n) -> p k n", k=KD)[:, :, j * NP2:(j + 1) * NP2],
            wgt_d[:, j * NP2:(j + 1) * NP2].rearrange("(k p) n -> p k n", p=128)),
          waits=E1, inc=s_wgp[j], inc_n=16)
    for j in range(8):
        A("gpsimd", lambda e, j=j: e.dma_start(
            wut.ap().rearrange("p (k n) -> p k n", k=KD)[:, :, j * NP2:(j + 1) * NP2],
            wut_d[:, j * NP2:(j + 1) * NP2].rearrange("(k p) n -> p k n", p=128)),
          waits=[(s_t, p1_pz_done)] + MP("st_xn", NT - 1), inc=s_wup[j], inc_n=16)
    vp2_scl(NT - 1)
    p1_r2_done = ms[("s_r2", NT - 1)]

    first2 = {"tensor": [(s_s, ms[("s_r1", NT - 1)])],
              "tensor_u": [(s_s, p1_sigz_done)],
              "vector": [], "scalar": [], "sync": []}

    def emit_sg(i, m):
        stw = [(s_shA, 16 * i)] if m < 12 else [(s_shB, 16 * i)]
        A("scalar", lambda e, m=m: e.activation(
            hbuf.ap()[:, m * T:(m + 1) * T],
            pp[0].ap()[:, (m % 2) * 512:(m % 2 + 1) * 512], AF.Silu, scale=S127),
          waits=first2["scalar"] + [(s_t, ms[("t_g", i, m)])]
                + [(s, v) for (s, v) in stw if v > 0], inc=s_s)
        first2["scalar"] = []
        ms[("s_sg", i, m)] = s_s.count

    def emit_vh(i, m):
        A("vector", lambda e, m=m: e.scalar_tensor_tensor(
            hbuf.ap()[:, m * T:(m + 1) * T], hbuf.ap()[:, m * T:(m + 1) * T],
            s2c.ap()[:, :1], pp[1].ap()[:, (m % 2) * 512:(m % 2 + 1) * 512],
            OP.mult, OP.mult),
          waits=first2["vector"] + [(s_s, ms[("s_sg", i, m)]), (s_t, ms[("t_u", i, m)])]
                + ([(s_v, M("v_chainc", NT - 1, m - 8))]
                   if (i == 0 and m >= 8) else []),
          inc=s_v)
        first2["vector"] = []
        ms[("v_h", i, m)] = s_v.count

    for i in range(NT):
        sl = i % 2
        if i > 0:
            A("gpsimd", lambda e, i=i, sl=sl: e.dma_start(
                xn2[sl].ap()[:, :], scr_xn[i]),
              waits=[(s_t, M("t_u", i - 2, CH - 1)), (s_sxn[sl], 16 * (i // 2 + 1))]
                    + (E2 if i < 2 else []),
              inc=s_xn2[sl], inc_n=16)
            ms[("ld_xn2", i)] = (s_xn2[sl], s_xn2[sl].count)

        if 2 <= i <= 4:
            jj = i - 2
            E4 = [(s_v, M("v_chain", NT - 1)), (s_syg, 16 * NT)] \
                 + ([(s_s, p1_r2_done)] if jj == 2 else [])
            A("gpsimd", lambda e, jj=jj: e.dma_start(
                wdtp[jj].ap()[:, :],
                wdt_d[jj * 768:(jj + 1) * 768, :].rearrange("(k p) n -> p k n", p=128)),
              waits=E4, inc=s_wdp[jj], inc_n=16)
            if jj == 0:
                A("scalar", lambda e: e.dma_start(
                    h3[0].ap()[:, :], scr_h[0]),
                  waits=[(s_v, M("v_chain", NT - 1)), (s_syg, 16 * NT),
                         (s_shA, 16), (s_shB, 16)],
                  inc=s_h3[0], inc_n=16)
                ms[("ld_h3", 0)] = (s_h3[0], s_h3[0].count)

        def emit_gate(m):
            gw = ([ms[("ld_xn2", i)]] if m == 0 else []) + [(s_wgp[m // 3], 16)] \
                 + [(s_s, M("s_sg", i, m - 2) if m >= 2 else M("s_sg", i - 1, CH - 2 + m))]
            for k in range(KD):
                A("tensor", lambda e, m=m, k=k, sl=sl: e.matmul(
                    pp[0].ap()[:, (m % 2) * 512:(m % 2 + 1) * 512],
                    wgt.ap()[:, k * 3072 + m * 128:k * 3072 + (m + 1) * 128],
                    xn2[sl].ap()[:, k * T:(k + 1) * T],
                    start=(k == 0), stop=(k == KD - 1)),
                  waits=(first2["tensor"] + gw) if k == 0 else [],
                  inc=(s_t if k == KD - 1 else None))
            first2["tensor"] = []
            ms[("t_g", i, m)] = s_t.count

        def emit_up(m):
            uw = ((first2["tensor_u"] if i == 0 else []) if m == 0 else []) \
                 + [(s_wup[m // 3], 16)] \
                 + [(s_v, M("v_h", i, m - 2) if m >= 2 else M("v_h", i - 1, CH - 2 + m))]
            for k in range(KD):
                A("tensor", lambda e, m=m, k=k, sl=sl: e.matmul(
                    pp[1].ap()[:, (m % 2) * 512:(m % 2 + 1) * 512],
                    wut.ap()[:, k * 3072 + m * 128:k * 3072 + (m + 1) * 128],
                    xn2[sl].ap()[:, k * T:(k + 1) * T],
                    start=(k == 0), stop=(k == KD - 1)),
                  waits=uw if k == 0 else [], inc=(s_t if k == KD - 1 else None))
            ms[("t_u", i, m)] = s_t.count

        if i == 0 or i == NT - 1:
            for m in range(CH):
                if i == 0 and m < CE:
                    emit_chain_c(NT - 1, m)
                emit_gate(m)
                emit_sg(i, m)
            for m in range(CH):
                emit_up(m)
                emit_vh(i, m)
                if m == 11:
                    A("sync", lambda e, i=i: e.dma_start(
                        scr_h[i][:, 0:12 * T], hbuf.ap()[:, 0:12 * T]),
                      waits=[(s_v, ms[("v_h", i, 11)])], inc=s_shA, inc_n=16)
                    ms[("st_hA", i)] = (s_shA, s_shA.count)
            A("sync", lambda e, i=i: e.dma_start(
                scr_h[i][:, 12 * T:], hbuf.ap()[:, 12 * T:]),
              waits=[(s_v, ms[("v_h", i, CH - 1)])], inc=s_shB, inc_n=16)
            ms[("st_hB", i)] = (s_shB, s_shB.count)
            continue

        for m in range(CH):
            emit_gate(m)
            emit_up(m)
            emit_sg(i, m)
            emit_vh(i, m)
            if m == 11:
                A("sync", lambda e, i=i: e.dma_start(
                    scr_h[i][:, 0:12 * T], hbuf.ap()[:, 0:12 * T]),
                  waits=[(s_v, ms[("v_h", i, 11)])], inc=s_shA, inc_n=16)
                ms[("st_hA", i)] = (s_shA, s_shA.count)
        A("sync", lambda e, i=i: e.dma_start(
            scr_h[i][:, 12 * T:], hbuf.ap()[:, 12 * T:]),
          waits=[(s_v, ms[("v_h", i, CH - 1)])], inc=s_shB, inc_n=16)
        ms[("st_hB", i)] = (s_shB, s_shB.count)

    p2_g_done = ms[("t_g", NT - 1, CH - 1)]
    p2_u_done = ms[("t_u", NT - 1, CH - 1)]
    p2_h_done = ms[("v_h", NT - 1, CH - 1)]
    p2_sg_done = ms[("s_sg", NT - 1, CH - 1)]
    shA_total = s_shA.count
    shB_total = s_shB.count
    syg_total = s_syg.count

    for j in range(2):
        A("gpsimd", lambda e, j=j: e.dma_start(
            woutb.ap().rearrange("p (k n) -> p k n", k=CE)[:, j * 8:(j + 1) * 8, :],
            wout[j * 1024:(j + 1) * 1024, :].rearrange("(k p) n -> p k n", p=128)),
          waits=[(s_t, p2_g_done)], inc=s_wop[j], inc_n=16)
    A("gpsimd", lambda e: e.dma_start(
        wdtp[3].ap()[:, :],
        wdt_d[3 * 768:4 * 768, :].rearrange("(k p) n -> p k n", p=128)),
      waits=[(s_t, p2_g_done)], inc=s_wdp[3], inc_n=16)

    first3 = {"tensor": [(s_s, p2_sg_done), (s_v, p2_h_done)],
              "vector": [(s_shA, shA_total), (s_shB, shB_total)], "sync": []}

    for i in range(NT):
        sl = i % 2
        if i > 0:
            A("gpsimd", lambda e, i=i, sl=sl: e.dma_start(h3[sl].ap()[:, :], scr_h[i]),
              waits=[(s_t, M("t_down", i - 2, 3)),
                     (s_shA, min(16 * (i + 1), shA_total)), (s_shB, 16 * (i + 1))]
                    + ([(s_t, p2_u_done)] if i < 3 else []),
              inc=s_h3[sl], inc_n=16)
            ms[("ld_h3", i)] = (s_h3[sl], s_h3[sl].count)
        A("gpsimd", lambda e, i=i, sl=sl: e.dma_start(
            x3[sl].ap().rearrange("p (s d) -> p s d", s=4),
            xnat[i * T:(i + 1) * T, :].rearrange("(s p) d -> p s d", p=128)),
          waits=[(s_v, M("v_evac3", i - 2, 3))]
                + ([[(s_t, p2_u_done)], [(s_v, p2_h_done), (s_shA, shA_total)]][i]
                   if i < 2 else []),
          inc=s_x3[sl], inc_n=16)
        ms[("ld_x3", i)] = (s_x3[sl], s_x3[sl].count)
        A("gpsimd", lambda e, i=i: e.dma_start(yg3.ap()[:, :], scr_yg[i]),
          waits=[(s_t, M("t_acc", i - 1, 3)), (s_syg, 16 * (i + 1))]
                + ([(s_t, p2_u_done)] if i == 0 else []),
          inc=s_yg3, inc_n=16)
        ms[("ld_yg3", i)] = (s_yg3, s_yg3.count)

        dorder = ([(k, s) for k in range(CH) for s in range(4)] if i == 0
                  else [(k, s) for s in range(4) for k in range(CH)])
        for k, s in dorder:
            w0 = ([(s_wdp[k // 6], 16)] if k % 6 == 0 else [])
            if k == 0:
                w0 = [ms[("ld_h3", i)], (s_v, M("v_evac3", i - 1, s))] + w0
                if i == 0 and s == 0:
                    w0 = first3["tensor"] + w0
            for half in range(2):
                A("tensor", lambda e, s=s, k=k, half=half, sl=sl: e.matmul(
                    pp[s].ap()[:, half * 512:(half + 1) * 512],
                    h3[sl].ap()[:, k * T + s * 128:k * T + (s + 1) * 128],
                    wdtp[k // 6].ap()[:, (k % 6) * 1024 + half * 512:(k % 6) * 1024 + (half + 1) * 512],
                    start=(k == 0), stop=False, skip_group_check=True),
                  waits=w0 if half == 0 else [],
                  inc=(s_t if (k == CH - 1 and half == 1) else None))
                w0 = []
            if k == CH - 1:
                ms[("t_down", i, s)] = s_t.count
        aorder = ([(k, s) for k in range(CE) for s in range(4)] if i == 0
                  else [(k, s) for s in range(4) for k in range(CE)])
        for k, s in aorder:
            w0 = ([ms[("ld_yg3", i)]] if k == 0 else []) \
                 + ([(s_wop[k // 8], 16)] if k % 8 == 0 else [])
            for half in range(2):
                A("tensor", lambda e, s=s, k=k, half=half: e.matmul(
                    pp[s].ap()[:, half * 512:(half + 1) * 512],
                    yg3.ap()[:, k * T + s * 128:k * T + (s + 1) * 128],
                    woutb.ap()[:, k * 1024 + half * 512:k * 1024 + (half + 1) * 512],
                    start=False, stop=(k == CE - 1), skip_group_check=True),
                  waits=w0 if half == 0 else [],
                  inc=(s_t if (k == CE - 1 and half == 1) else None))
                w0 = []
            if k == CE - 1:
                ms[("t_acc", i, s)] = s_t.count
        for s in range(4):
            A("vector", lambda e, s=s, sl=sl: e.tensor_add(
                outb.ap()[:, s * 1024:(s + 1) * 1024], pp[s].ap()[:, :],
                x3[sl].ap()[:, s * 1024:(s + 1) * 1024]),
              waits=first3["vector"] + [(s_t, ms[("t_acc", i, s)]),
                                        ms[("ld_x3", i)]]
                    + MP("st_out", i - 1, s),
              inc=s_v)
            first3["vector"] = []
            ms[("v_evac3", i, s)] = s_v.count
            A("sync", lambda e, i=i, s=s: e.dma_start(
                out_d[i * T + s * 128:i * T + (s + 1) * 128, :],
                outb.ap()[:, s * 1024:(s + 1) * 1024]),
              waits=first3["sync"] + [(s_v, ms[("v_evac3", i, s)])],
              inc=s_sout, inc_n=16)
            first3["sync"] = []
            ms[("st_out", i, s)] = (s_sout, s_sout.count)

    final_st = s_sout.count
    A("sync", lambda e: e.wait_ge(s_sout.h, final_st))

    with nc.Block() as block:
        block.tensor(_runner(sched, "tensor"))
        block.vector(_runner(sched, "vector"))
        block.scalar(_runner(sched, "scalar"))
        block.sync(_runner(sched, "sync"))
        block.gpsimd(_runner(sched, "gpsimd"))
    es.close()
    return nc


_NC_CACHE = {}


def kernel(L0_attn_x, L0_attn_Win, L0_attn_conv_w, L0_attn_conv_b, L0_attn_Wx,
           L0_attn_Wdt, L0_attn_bdt, L0_attn_A_log, L0_attn_D, L0_attn_ln_g,
           L0_attn_ln_b, L0_attn_Wout, L0_attn_res_scale, L0_ffn_w_gate,
           L0_ffn_w_up, L0_ffn_w_down, **_unused):
    x = np.asarray(L0_attn_x, dtype=np.float32)
    Win = np.ascontiguousarray(np.asarray(L0_attn_Win, np.float32)[:, :4096])
    Wout = np.ascontiguousarray(np.asarray(L0_attn_Wout, np.float32))
    wgT = np.ascontiguousarray(np.asarray(L0_ffn_w_gate).T).astype(BF)
    wuT = np.ascontiguousarray(np.asarray(L0_ffn_w_up).T).astype(BF)
    wdT = np.ascontiguousarray(np.asarray(L0_ffn_w_down).T).astype(BF)
    D = np.asarray(L0_attn_D, np.float32)
    cwv = np.asarray(L0_attn_conv_w, np.float32)
    cbv = np.asarray(L0_attn_conv_b, np.float32)
    lngv = np.asarray(L0_attn_ln_g, np.float32)
    lnbv = np.asarray(L0_attn_ln_b, np.float32)
    rsv = np.asarray(L0_attn_res_scale, np.float32)

    dcol = np.ascontiguousarray(D.reshape(CE, 128).T)
    cw = np.ascontiguousarray(cwv.reshape(CE, 128, 4).transpose(1, 0, 2).reshape(128, CE * 4))
    cb = np.ascontiguousarray(cbv.reshape(CE, 128).T)
    lng = np.ascontiguousarray(lngv.reshape(CE, 128).T)
    lnb = np.ascontiguousarray(lnbv.reshape(CE, 128).T)
    rs = np.full((128, 1), rsv[0], np.float32)

    in_maps = []
    for c in range(8):
        b, hh = c // 2, c % 2
        s0 = hh * NTOK
        xseq = x[b]
        xTh = np.zeros((1024, HALO + NTOK), np.float32)
        xTh[:, HALO:] = xseq[s0:s0 + NTOK].T
        if s0 > 0:
            xTh[:, HALO - 3:HALO] = xseq[s0 - 3:s0].T
        in_maps.append({
            "xTh": np.ascontiguousarray(xTh),
            "xnat": np.ascontiguousarray(xseq[s0:s0 + NTOK]),
            "win": Win, "wout": Wout, "wgt": wgT, "wut": wuT, "wdt": wdT,
            "dcol": dcol, "cw": cw, "cb": cb, "lng": lng, "lnb": lnb, "rs": rs,
        })

    nc = build_nc()
    res = run_bass_kernel_spmd(nc, in_maps, core_ids=list(range(8)))
    out = np.empty((4, 8192, 1024), np.float32)
    for c in range(8):
        b, hh = c // 2, c % 2
        out[b, hh * NTOK:(hh + 1) * NTOK] = res.results[c]["out"]
    return out


if __name__ == "__main__":
    rng = np.random.RandomState(0)
    print("smoke build only")
    build_nc()
    print("build ok")


# version 32
# speedup vs baseline: 1.2281x; 1.0023x over previous
import numpy as np
import ml_dtypes
import concourse.bass as bass
import concourse.mybir as mybir
from concourse.bass_utils import run_bass_kernel_spmd
from contextlib import ExitStack

BF16 = mybir.dt.bfloat16
F32 = mybir.dt.float32
AF = mybir.ActivationFunctionType
OP = mybir.AluOpType
BF = ml_dtypes.bfloat16

NT = 8
T = 512
KD = 8
CE = 16
CH = 24
MO = 8
NTOK = NT * T
HALO = 32
S127 = float(np.float32(1.0 / 127.0))
S127SQ = float(np.float32(1.0 / 127.0) * np.float32(1.0 / 127.0))
EPS = 1e-5

ENGINES = ("tensor", "vector", "scalar", "sync", "gpsimd")


class Sem:
    def __init__(self, handle):
        self.h = handle
        self.count = 0


class Sched:
    def __init__(self):
        self.ops = {e: [] for e in ENGINES}
        self.autosync = {}

    def add(self, eng, fn, waits=(), inc=None, inc_n=1):
        w = [(s, v) for (s, v) in waits if v > 0]
        aus = self.autosync.get(eng)
        if aus is not None:
            if aus.count > 0:
                w = w + [(aus, aus.count)]
            if inc is None:
                inc = aus
        incs = []
        val = None
        if inc is not None:
            inc.count += inc_n
            val = inc.count
            incs.append((inc, inc_n))
        self.ops[eng].append((w, fn, incs))
        return val


def _runner(sched, name):
    def go(eng):
        lastw = {}
        for waits, fn, incs in sched.ops[name]:
            for sem, val in waits:
                k = id(sem)
                if val > lastw.get(k, 0):
                    eng.wait_ge(sem.h, val)
                    lastw[k] = val
            inst = fn(eng)
            for sem, n in incs:
                inst.then_inc(sem.h, n)
    return go


def build_nc(debug=False):
    nc = bass.Bass("TRN2", target_bir_lowering=False)

    xTh = nc.dram_tensor("xTh", [1024, HALO + NTOK], F32, kind="ExternalInput").ap()
    xnat = nc.dram_tensor("xnat", [NTOK, 1024], F32, kind="ExternalInput").ap()
    win = nc.dram_tensor("win", [1024, 4096], F32, kind="ExternalInput").ap()
    wout = nc.dram_tensor("wout", [2048, 1024], F32, kind="ExternalInput").ap()
    wgt_d = nc.dram_tensor("wgt", [1024, 3072], BF16, kind="ExternalInput").ap()
    wut_d = nc.dram_tensor("wut", [1024, 3072], BF16, kind="ExternalInput").ap()
    wdt_d = nc.dram_tensor("wdt", [3072, 1024], BF16, kind="ExternalInput").ap()
    dcol_d = nc.dram_tensor("dcol", [128, CE], F32, kind="ExternalInput").ap()
    cw_d = nc.dram_tensor("cw", [128, CE * 4], F32, kind="ExternalInput").ap()
    cb_d = nc.dram_tensor("cb", [128, CE], F32, kind="ExternalInput").ap()
    lng_d = nc.dram_tensor("lng", [128, CE], F32, kind="ExternalInput").ap()
    lnb_d = nc.dram_tensor("lnb", [128, CE], F32, kind="ExternalInput").ap()
    rs_d = nc.dram_tensor("rs", [128, 1], F32, kind="ExternalInput").ap()
    out_d = nc.dram_tensor("out", [NTOK, 1024], F32, kind="ExternalOutput").ap()

    scr_kind = "ExternalOutput" if debug else "Internal"
    scr_xn = nc.dram_tensor("scr_xn", [NT, 128, KD * T], BF16, kind=scr_kind).ap()
    scr_yg = nc.dram_tensor("scr_yg", [NT, 128, CE * T], BF16, kind=scr_kind).ap()
    scr_h = nc.dram_tensor("scr_h", [NT, 128, CH * T], BF16, kind=scr_kind).ap()

    avgb = nc.alloc_sbuf_tensor("avgb", [128, 128], BF16)
    Db = nc.alloc_sbuf_tensor("Db", [128, CE * 128], BF16)
    D2b = nc.alloc_sbuf_tensor("D2b", [128, CE * 128], BF16)
    dcol = nc.alloc_sbuf_tensor("dcolS", [128, CE], F32)
    cw = nc.alloc_sbuf_tensor("cwS", [128, CE * 4], F32)
    cb = nc.alloc_sbuf_tensor("cbS", [128, CE], F32)
    lng = nc.alloc_sbuf_tensor("lngS", [128, CE], F32)
    lnb = nc.alloc_sbuf_tensor("lnbS", [128, CE], F32)
    gq = nc.alloc_sbuf_tensor("gq", [128, CE], F32)
    bq = nc.alloc_sbuf_tensor("bq", [128, CE], F32)
    rs = nc.alloc_sbuf_tensor("rsS", [128, 1], F32)
    epsc = nc.alloc_sbuf_tensor("epsc", [128, 1], F32)
    onec = nc.alloc_sbuf_tensor("onec", [128, 1], F32)
    s2c = nc.alloc_sbuf_tensor("s2c", [128, 1], F32)
    ARENA = 202816
    arena_buf = nc.alloc_sbuf_tensor("arena", [128, ARENA], mybir.dt.uint8)
    base = nc.lookup_mloc(arena_buf).addr

    def at(name, shape, dtype, off):
        nbytes = int(np.prod(shape[1:])) * (2 if dtype == BF16 else 4)
        assert off + nbytes <= ARENA, (name, off, nbytes)
        return nc.alloc_sbuf_tensor_at(name, list(shape), dtype, offset=base + off)

    xb = [at("xb0", [128, KD * T], BF16, 0),
          at("xb1", [128, KD * T], BF16, 8192)]
    x2f = at("x2f", [128, KD * T], BF16, 16384)
    xp = at("xp", [128, CE * (T + HALO)], BF16, 24576)
    xs2r = at("xs2r", [128, 2 * T], BF16, 41984)
    diag = at("diag", [128, CE * 4 * 128], BF16, 44032)
    mu1 = at("mu1", [128, T], BF16, 60416)
    r1 = at("r1", [128, T], BF16, 61440)
    tmf = at("tmf", [128, T], BF16, 62464)
    v1t = at("v1t", [128, T], F32, 63488)
    xn = [at("xn0", [128, KD * T], BF16, 65536),
          at("xn1", [128, KD * T], BF16, 73728)]
    winb = at("winb", [128, KD * 4096], BF16, 81920)
    xs = at("xs", [128, CE * T], BF16, 147456)
    sigz = at("sigz", [128, CE * T], BF16, 163840)
    yg = at("yg", [128, CE * T], BF16, 180224)
    tA = at("tA", [128, T], BF16, 196608)
    tB = at("tB", [128, T], BF16, 197632)
    mu2 = at("mu2", [128, T], BF16, 198656)
    r2 = at("r2", [128, T], BF16, 199680)
    sS1 = at("sS1", [128, T], F32, 200704)
    o = 180224
    def at2(name, shape, dtype):
        nonlocal o
        t = at(name, shape, dtype, o)
        o += int(np.prod(shape[1:])) * (2 if dtype == BF16 else 4)
        return t
    xbh = at2("xbh", [128, KD * HALO], BF16)
    xbh2 = at2("xbh2", [128, KD * HALO], BF16)
    xnh = at2("xnh", [128, KD * HALO], BF16)
    muh = at2("muh", [128, HALO], F32)
    varh = at2("varh", [128, HALO], F32)
    rh = at2("rh", [128, HALO], F32)
    sSh = at2("sSh", [128, HALO], F32)
    tmh = at2("tmh", [128, HALO], F32)
    identb = at2("identb", [128, 128], BF16)
    itmp = at2("itmp", [128, 128], BF16)
    wgt = at("wgtS", [128, KD * 3072], BF16, 0)
    xn2 = [at("xn2_0", [128, KD * T], BF16, 49152),
           at("xn2_1", [128, KD * T], BF16, 57344)]
    wut = at("wutS", [128, KD * 3072], BF16, 65536)
    hbuf = at("hbuf", [128, CH * T], BF16, 116736)
    woutb = at("woutb", [128, CE * 1024], BF16, 0)
    wdt3 = at("wdt3", [128, 6 * 1024], BF16, 32768)
    yg3 = at("yg3", [128, CE * T], BF16, 45056)
    h3 = [at("h3_0", [128, CH * T], BF16, 141312),
          at("h3_1", [128, CH * T], BF16, 61440)]
    x3 = [at("x3_0", [128, 4 * 1024], F32, 86016),
          at("x3_1", [128, 4 * 1024], F32, 102400)]
    outb = at("outb", [128, 4 * 1024], F32, 118784)
    wdtp = [at(f"wdtp{j}", [128, 6 * 1024], BF16, 165888 + 12288 * j)
            for j in range(3)] + [wdt3]

    pp = [nc.alloc_psum_tensor(f"pp{j}", [128, 1024], F32) for j in range(4)]

    sched = Sched()
    es = ExitStack()

    def mksem(nm):
        return Sem(es.enter_context(nc.semaphore(nm)))
    s_t = mksem("s_t")
    s_v = mksem("s_v")
    s_s = mksem("s_s")
    s_g = mksem("s_g")
    s_cst = mksem("s_cst")
    s_halo = mksem("s_halo")
    s_winp = [mksem(f"s_winp{j}") for j in range(16)]
    s_xb = [mksem("s_xb0"), mksem("s_xb1")]
    s_xn2 = [mksem("s_xn2_0"), mksem("s_xn2_1")]
    s_yg3 = mksem("s_yg3")
    s_h3 = [mksem("s_h3_0"), mksem("s_h3_1")]
    s_x3 = [mksem("s_x3_0"), mksem("s_x3_1")]
    s_sxn = [mksem("s_sxn0"), mksem("s_sxn1")]
    s_syg = mksem("s_syg")
    s_shA = mksem("s_shA")
    s_shB = mksem("s_shB")
    s_sout = mksem("s_sout")
    s_dbg = mksem("s_dbg")
    s_wgp = [mksem(f"s_wgp{j}") for j in range(8)]
    s_wup = [mksem(f"s_wup{j}") for j in range(8)]
    s_wdp = [mksem(f"s_wdp{j}") for j in range(4)]
    s_wop = [mksem(f"s_wop{j}") for j in range(2)]
    sched.autosync = {"vector": s_v}

    A = sched.add

    for tile, valv in ((avgb, 1.0 / 1024.0), (epsc, EPS), (onec, 1.0), (s2c, S127SQ)):
        A("gpsimd", lambda e, t=tile, v=valv: e.memset(t.ap()[:, :], v), inc=s_g)
    pro_g = s_g.count
    A("gpsimd", lambda e: e.dma_start(
        xbh.ap().rearrange("p (k t) -> p k t", t=HALO),
        xTh[:, 0:HALO].rearrange("(k p) t -> p k t", p=128)), inc=s_halo, inc_n=16)
    A("gpsimd", lambda e: e.dma_start(
        xb[0].ap().rearrange("p (k t) -> p k t", k=KD),
        xTh[:, HALO: HALO + T].rearrange("(k p) t -> p k t", p=128)),
      inc=s_xb[0], inc_n=16)
    ms0_ld_xb0 = (s_xb[0], s_xb[0].count)
    A("gpsimd", lambda e: e.dma_start(
        xb[1].ap().rearrange("p (k t) -> p k t", k=KD),
        xTh[:, HALO + T: HALO + 2 * T].rearrange("(k p) t -> p k t", p=128)),
      inc=s_xb[1], inc_n=16)
    ms0_ld_xb1 = (s_xb[1], s_xb[1].count)
    A("gpsimd", lambda e: e.iota(itmp.ap()[:, :], [[1, 128]], channel_multiplier=-1,
                                 allow_small_or_imprecise_dtypes=True), inc=s_g)
    pro_g2 = s_g.count
    for name, dst, src in (
        ("dcol", dcol, dcol_d), ("cw", cw, cw_d), ("cb", cb, cb_d),
        ("lng", lng, lng_d), ("lnb", lnb, lnb_d), ("rs", rs, rs_d),
    ):
        A("sync", lambda e, d=dst, s=src: e.dma_start(d.ap()[:, :], s),
          inc=s_cst, inc_n=16)
    ld_cst = s_cst.count
    for j in range(8):
        A("gpsimd", lambda e, j=j: e.dma_start(
            winb.ap().rearrange("p (k n) -> p k n", k=KD)[:, :, j * 512:(j + 1) * 512],
            win[:, j * 512:(j + 1) * 512].rearrange("(k p) n -> p k n", p=128)),
          inc=s_winp[j], inc_n=16)

    def emit_const_derivs():
        A("vector", lambda e: e.tensor_single_scalar(identb.ap()[:, :], itmp.ap()[:, :],
                                                     0.0, OP.is_equal),
          waits=[(s_g, pro_g2), (s_cst, ld_cst)])
        pdc, dbp = [], []
        for c in range(CE):
            for k in range(4):
                A("vector", lambda e, c=c, k=k: e.tensor_scalar_mul(
                    diag.ap()[:, (c * 4 + k) * 128:(c * 4 + k + 1) * 128],
                    identb.ap()[:, :], cw.ap()[:, c * 4 + k:c * 4 + k + 1]))
            pdc.append(s_v.count)
            A("vector", lambda e, c=c: e.tensor_scalar(
                Db.ap()[:, c * 128:(c + 1) * 128], avgb.ap()[:, :],
                dcol.ap()[:, c:c + 1], 0.5, OP.mult, OP.mult),
              waits=[(s_g, pro_g)], inc=s_v)
            A("vector", lambda e, c=c: e.tensor_scalar_mul(
                D2b.ap()[:, c * 128:(c + 1) * 128],
                Db.ap()[:, c * 128:(c + 1) * 128], dcol.ap()[:, c:c + 1]), inc=s_v)
            dbp.append(s_v.count)
        A("vector", lambda e: e.tensor_scalar_mul(gq.ap()[:, :], lng.ap()[:, :], rs.ap()[:, :1]), inc=s_v)
        A("vector", lambda e: e.tensor_scalar_mul(bq.ap()[:, :], lnb.ap()[:, :], rs.ap()[:, :1]), inc=s_v)
        return pdc, dbp, s_v.count

    def ln_stats(engv, psum_pair, mu_sl, r_sl, n, wait_t_val, pre_v=()):
        tiny = True
        A("vector", lambda e: e.tensor_copy(mu_sl[:, 0:n], psum_pair.ap()[:, 0:n]),
          waits=[(s_t, wait_t_val)] + [w for w in pre_v if w[1] > 0], inc=s_v)
        cpv = s_v.count
        A("vector", lambda e: e.tensor_mul(tmf.ap()[:, 0:n] if n == T else tmh.ap()[:, 0:n],
                                           mu_sl[:, 0:n], mu_sl[:, 0:n]),
          waits=([(s_v, cpv)] if tiny else []), inc=s_v)
        mlv = s_v.count
        tm = tmf.ap() if n == T else tmh.ap()
        vt = v1t.ap() if n == T else varh.ap()
        sS = sS1.ap() if n == T else sSh.ap()
        A("vector", lambda e: e.tensor_sub(vt[:, 0:n], psum_pair.ap()[:, 512:512 + n], tm[:, 0:n]),
          waits=([(s_v, mlv)] if tiny else []), inc=s_v)
        v_done = s_v.count
        A("scalar", lambda e: e.activation(sS[:, 0:n], vt[:, 0:n], AF.Ln,
                                           bias=epsc.ap()[:, :1], scale=onec.ap()[:, :1]),
          waits=[(s_v, v_done), (s_g, pro_g)], inc=s_s)
        lnv = s_s.count
        A("scalar", lambda e: e.activation(r_sl[:, 0:n], sS[:, 0:n], AF.Exp, scale=-0.5),
          waits=([(s_s, lnv)] if tiny else []), inc=s_s)
        return v_done, s_s.count

    n = HALO
    A("vector", lambda e: e.tensor_mul(xbh2.ap()[:, :], xbh.ap()[:, :], xbh.ap()[:, :]),
      waits=[(s_halo, 16)], inc=s_v)
    v_x2h = s_v.count
    for k in range(KD):
        A("tensor", lambda e, k=k: e.matmul(
            pp[2].ap()[:, 0:n], avgb.ap()[:, :], xbh.ap()[:, k * n:(k + 1) * n],
            start=(k == 0), stop=(k == KD - 1)),
          waits=[(s_halo, 16), (s_g, pro_g)])
        A("tensor", lambda e, k=k: e.matmul(
            pp[2].ap()[:, 512:512 + n], avgb.ap()[:, :], xbh2.ap()[:, k * n:(k + 1) * n],
            start=(k == 0), stop=(k == KD - 1)),
          waits=[(s_v, v_x2h)], inc=(s_t if k == KD - 1 else None))
    t_stats_h = s_t.count
    v_done_h, s_rh = ln_stats("v", pp[2], muh.ap(), rh.ap(), HALO, t_stats_h)
    for k in range(KD):
        A("vector", lambda e, k=k: e.tensor_sub(
            tA.ap()[:, 0:n], xbh.ap()[:, k * n:(k + 1) * n], muh.ap()[:, 0:n]),
          waits=[(s_s, s_rh)], inc=s_v)
        sbv = s_v.count
        A("vector", lambda e, k=k: e.tensor_mul(
            xnh.ap()[:, k * n:(k + 1) * n], tA.ap()[:, 0:n], rh.ap()[:, 0:n]),
          waits=[(s_v, sbv)], inc=s_v)
    v_xnh = s_v.count
    ev_h = []
    for m in range(CE):
        for k in range(KD):
            A("tensor", lambda e, m=m, k=k: e.matmul(
                pp[3].ap()[:, (m % 2) * 512:(m % 2) * 512 + n],
                winb.ap()[:, k * 4096 + m * 128:k * 4096 + (m + 1) * 128],
                xnh.ap()[:, k * n:(k + 1) * n],
                start=(k == 0), stop=(k == KD - 1)),
              waits=([(s_v, v_xnh), (s_winp[m // 4], 16)] if k == 0 else [])
                    + ([(s_s, ev_h[m - 2])] if (k == 0 and m >= 2) else []),
              inc=(s_t if k == KD - 1 else None))
        t_m = s_t.count
        A("scalar", lambda e, m=m: e.copy(
            xp.ap()[:, m * (T + HALO):m * (T + HALO) + HALO],
            pp[3].ap()[:, (m % 2) * 512:(m % 2) * 512 + n]),
          waits=[(s_t, t_m)], inc=s_s)
        ev_h.append(s_s.count)

    if debug:
        for nm, buf, wl in (
            ("dbg_xbh", xbh, [(s_halo, 16)]),
            ("dbg_xbh2", xbh2, [(s_v, v_x2h)]),
            ("dbg_muh", muh, [(s_s, s_rh)]),
            ("dbg_varh", varh, [(s_s, s_rh)]),
            ("dbg_rh", rh, [(s_s, s_rh)]),
            ("dbg_xnh", xnh, [(s_v, v_xnh)]),
        ):
            shp = [buf.shape[0], buf.shape[1]]
            dt_ = nc.dram_tensor(nm, shp, buf.dtype, kind="ExternalOutput").ap()
            A("sync", lambda e, d=dt_, b=buf: e.dma_start(d, b.ap()[:, :]),
              waits=wl, inc=s_dbg, inc_n=16)

    ms = {}
    ms[("ld_xb", 0)] = ms0_ld_xb0
    ms[("ld_xb", 1)] = ms0_ld_xb1

    def M(name, i, *rest):
        return ms.get((name, i) + rest, 0)

    def MP(name, i, *rest):
        p = ms.get((name, i) + rest)
        return [p] if p else []

    def emit_ld_xb(j):
        if j >= NT:
            return
        if ("ld_xb", j) in ms:
            return
        A("gpsimd", lambda e, j=j: e.dma_start(
            xb[j % 2].ap().rearrange("p (k t) -> p k t", k=KD),
            xTh[:, HALO + j * T: HALO + (j + 1) * T].rearrange("(k p) t -> p k t", p=128)),
          waits=[(s_v, M("v_xn", j - 2))], inc=s_xb[j % 2], inc_n=16)
        ms[("ld_xb", j)] = (s_xb[j % 2], s_xb[j % 2].count)

    def emit_x2(j, ks=None):
        if j >= NT:
            return
        for k in (range(KD) if ks is None else ks):
            A("gpsimd", lambda e, k=k, j=j: e.tensor_mul(
                x2f.ap()[:, k * T:(k + 1) * T],
                xb[j % 2].ap()[:, k * T:(k + 1) * T], xb[j % 2].ap()[:, k * T:(k + 1) * T]),
              waits=[ms[("ld_xb", j)], (s_t, M("t_s1b", j - 1, k))], inc=s_g)
            ms[("g_x2", j, k)] = s_g.count

    def emit_s1k(j, k):
        if j >= NT:
            return
        A("tensor", lambda e, k=k, j=j: e.matmul(
            pp[0].ap()[:, 0:T], avgb.ap()[:, :], xb[j % 2].ap()[:, k * T:(k + 1) * T],
            start=(k == 0), stop=(k == KD - 1)),
          waits=([(s_s, M("s_r1", j - 1)), ms[("ld_xb", j)]] if k == 0 else []))
        A("tensor", lambda e, k=k: e.matmul(
            pp[0].ap()[:, 512:512 + T], avgb.ap()[:, :],
            x2f.ap()[:, k * T:(k + 1) * T],
            start=(k == 0), stop=(k == KD - 1)),
          waits=[(s_g, ms[("g_x2", j, k)])], inc=s_t)
        ms[("t_s1b", j, k)] = s_t.count
        if k == KD - 1:
            ms[("t_stats1", j)] = s_t.count

    def emit_stats1(j):
        for k in range(KD):
            emit_s1k(j, k)

    def vp1_vec(j):
        if j >= NT:
            return
        A("vector", lambda e: e.tensor_copy(mu1.ap()[:, :], pp[0].ap()[:, 0:T]),
          waits=[(s_t, ms[("t_stats1", j)])], inc=s_v)
        A("vector", lambda e: e.tensor_mul(tmf.ap()[:, :], mu1.ap()[:, :], mu1.ap()[:, :]), inc=s_v)
        A("vector", lambda e: e.tensor_sub(pp[0].ap()[:, 0:T], pp[0].ap()[:, 512:1024],
                                           tmf.ap()[:, :]), inc=s_v)
        ms[("v_var1", j)] = s_v.count

    def vp1_scl(j):
        if j >= NT:
            return
        A("scalar", lambda e: e.activation(sS1.ap()[:, :], pp[0].ap()[:, 0:T], AF.Ln,
                                           bias=epsc.ap()[:, :1], scale=onec.ap()[:, :1]),
          waits=[(s_v, ms[("v_var1", j)]), (s_g, pro_g), (s_s, s_s.count)], inc=s_s)
        lnv = s_s.count
        A("scalar", lambda e: e.activation(r1.ap()[:, :], sS1.ap()[:, :], AF.Exp, scale=-0.5),
          waits=[(s_s, lnv), (s_v, M("v_xn", j - 1))], inc=s_s)
        ms[("s_r1", j)] = s_s.count

    def vp2_vec(j):
        A("vector", lambda e: e.tensor_copy(mu2.ap()[:, :], pp[2].ap()[:, 0:T]),
          waits=[(s_t, ms[("t_stats2", j)])], inc=s_v)
        A("vector", lambda e: e.tensor_mul(tmf.ap()[:, :], mu2.ap()[:, :], mu2.ap()[:, :]), inc=s_v)
        A("vector", lambda e: e.tensor_sub(pp[2].ap()[:, 0:T], pp[2].ap()[:, 512:1024],
                                           tmf.ap()[:, :]), inc=s_v)
        ms[("v_var2", j)] = s_v.count

    def vp2_scl(j):
        A("scalar", lambda e: e.activation(sS1.ap()[:, :], pp[2].ap()[:, 0:T], AF.Ln,
                                           bias=epsc.ap()[:, :1], scale=onec.ap()[:, :1]),
          waits=[(s_v, ms[("v_var2", j)]), (s_g, pro_g), (s_s, s_s.count)], inc=s_s)
        ms[("s_lnv2", j)] = s_s.count
        lnv = s_s.count
        A("scalar", lambda e: e.activation(r2.ap()[:, :], sS1.ap()[:, :], AF.Exp, scale=-0.5),
          waits=[(s_s, lnv), (s_v, M("v_chainc", j - 1, CE - 1))], inc=s_s)
        ms[("s_r2", j)] = s_s.count

    def emit_xn_part(j, part):
        if j >= NT:
            return
        for k in range(2 * part, 2 * part + 2):
            A("vector", lambda e, k=k, j=j: e.tensor_sub(
                tA.ap()[:, :], xb[j % 2].ap()[:, k * T:(k + 1) * T], mu1.ap()[:, :]),
              waits=[(s_s, ms[("s_r1", j)])]
                    + ((MP("st_xn", j - 2) + [(s_t, M("t_pz", j - 2, 2 * CE - 1))])
                       if k == 2 * part else []))
            A("vector", lambda e, k=k, j=j: e.tensor_mul(
                xn[j % 2].ap()[:, k * T:(k + 1) * T], tA.ap()[:, :], r1.ap()[:, :]))
        if part == 3:
            ms[("v_xn", j)] = s_v.count
            A("sync", lambda e, j=j: e.dma_start(scr_xn[j], xn[j % 2].ap()[:, :]),
              waits=[(s_v, ms[("v_xn", j)])], inc=s_sxn[j % 2], inc_n=16)
            ms[("st_xn", j)] = (s_sxn[j % 2], s_sxn[j % 2].count)

    def emit_chain_c(j, c):
        A("vector", lambda e, c=c: e.scalar_tensor_tensor(
            tA.ap()[:, :], xs.ap()[:, c * T:(c + 1) * T], dcol.ap()[:, c:c + 1],
            mu2.ap()[:, :], OP.mult, OP.subtract),
          waits=[(s_s, ms[("s_silu", j, c)])])
        ms[("v_ch1", j, c)] = s_v.count
        A("vector", lambda e, c=c: e.scalar_tensor_tensor(
            tB.ap()[:, :], tA.ap()[:, :], gq.ap()[:, c:c + 1],
            r2.ap()[:, :], OP.mult, OP.mult),
          waits=([(s_s, ms[("s_r2", j)])] if c == 0 else []))
        A("vector", lambda e, c=c: e.scalar_tensor_tensor(
            yg.ap()[:, c * T:(c + 1) * T], tB.ap()[:, :], bq.ap()[:, c:c + 1],
            sigz.ap()[:, c * T:(c + 1) * T], OP.add, OP.mult),
          waits=(MP("st_yg", j - 1) if c == 0 else []))
        ms[("v_chainc", j, c)] = s_v.count
        if c == CE - 1:
            ms[("v_chain", j)] = s_v.count
            A("sync", lambda e, j=j: e.dma_start(scr_yg[j], yg.ap()[:, :]),
              waits=[(s_v, ms[("v_chain", j)])], inc=s_syg, inc_n=16)
            ms[("st_yg", j)] = (s_syg, s_syg.count)

    emit_x2(0)
    emit_stats1(0)
    vp1_vec(0)
    vp1_scl(0)
    for p_ in range(4):
        emit_xn_part(0, p_)
    emit_x2(1)
    emit_stats1(1)
    vp1_vec(1)
    pro_diag_c, db_pair, pro_v = emit_const_derivs()
    pro_diag = pro_v

    for i in range(NT):
        sl = i % 2
        emit_ld_xb(i + 2)

        if i > 0:
            vp2_scl(i - 1)
        vp1_scl(i + 1)

        if i > 0:
            A("vector", lambda e: e.tensor_copy(
                xp.ap().rearrange("p (c w) -> p c w", c=CE)[:, :, HALO - 8:HALO],
                xp.ap().rearrange("p (c w) -> p c w", c=CE)[:, :, HALO + T - 8:HALO + T]),
              waits=[(s_s, M("s_evac", i - 1, 15)), (s_t, M("t_conv", i - 1, CE - 1))])
            ms[("v_halo", i)] = s_v.count

        def emit_sigz(c):
            src_pp = pp[1] if c % 4 < 2 else pp[3]
            half = (c % 4) % 2
            A("scalar", lambda e, c=c, src_pp=src_pp, half=half: e.activation(
                sigz.ap()[:, c * T:(c + 1) * T],
                src_pp.ap()[:, half * 512:(half + 1) * 512], AF.Sigmoid),
              waits=[(s_t, ms[("t_pz", i, CE + c)]),
                     (s_v, M("v_chainc", i - 1, c))], inc=s_s)
            ms[("s_sigz", i, c)] = s_s.count

        def emit_conv(c):
            base = c * (T + HALO) + (HALO - 3)
            w0 = [(s_s, ms[("s_evac", i, c)]),
                  (s_s, (M("s_silu", i, c - 2) if c >= 2 else
                         (M("s_sigz", i - 1, 14 + c) if i > 0 else ev_h[14 + c]))),
                  (s_v, pro_diag_c[c] if i == 0 else pro_diag)]
            if i > 0:
                w0 = w0 + [(s_v, M("v_halo", i))]
            for k in range(4):
                A("tensor", lambda e, c=c, k=k, base=base: e.matmul(
                    pp[3].ap()[:, (c % 2) * 512:(c % 2 + 1) * 512],
                    diag.ap()[:, (c * 4 + k) * 128:(c * 4 + k + 1) * 128],
                    xp.ap()[:, base + k:base + k + T],
                    start=(k == 0), stop=(k == 3)),
                  waits=w0 if k == 0 else [],
                  inc=(s_t if k == 3 else None))
            ms[("t_conv", i, c)] = s_t.count

        def emit_silu(c):
            A("scalar", lambda e, c=c: e.activation(
                xs.ap()[:, c * T:(c + 1) * T],
                pp[3].ap()[:, (c % 2) * 512:(c % 2 + 1) * 512],
                AF.Silu, bias=cb.ap()[:, c:c + 1], scale=onec.ap()[:, :1]),
              waits=[(s_t, ms[("t_conv", i, c)]),
                     (s_v, M("v_ch1", i - 1, c))], inc=s_s)
            ms[("s_silu", i, c)] = s_s.count

        def emit_xs2(c):
            A("gpsimd", lambda e, c=c: e.tensor_mul(
                xs2r.ap()[:, (c % 2) * T:(c % 2 + 1) * T],
                xs.ap()[:, c * T:(c + 1) * T], xs.ap()[:, c * T:(c + 1) * T]),
              waits=[(s_s, ms[("s_silu", i, c)]), (s_t, M("t_s2b", i, c - 2))],
              inc=s_g)
            ms[("g_xs2", i, c)] = s_g.count

        def emit_sv(c):
            A("tensor", lambda e, c=c: e.matmul(
                pp[2].ap()[:, 0:T], Db.ap()[:, c * 128:(c + 1) * 128],
                xs.ap()[:, c * T:(c + 1) * T], start=(c == 0), stop=(c == CE - 1)),
              waits=[(s_s, ms[("s_silu", i, c)]),
                     (s_v, db_pair[c] if i == 0 else 0)]
                    + (([(s_s, M("s_lnv2", i - 1))] if i > 0 else [(s_v, v_done_h)])
                       if c == 0 else []))
            A("tensor", lambda e, c=c: e.matmul(
                pp[2].ap()[:, 512:512 + T], D2b.ap()[:, c * 128:(c + 1) * 128],
                xs2r.ap()[:, (c % 2) * T:(c % 2 + 1) * T],
                start=(c == 0), stop=(c == CE - 1)),
              waits=[(s_g, ms[("g_xs2", i, c)])], inc=s_t)
            ms[("t_s2b", i, c)] = s_t.count
            if c == CE - 1:
                ms[("t_stats2", i)] = s_t.count

        def zbank(c):
            return (pp[1] if c % 4 < 2 else pp[3]), (c % 4) % 2
        for m in range(2 * CE):
            if 0 <= m - 18 < CE - 2:
                emit_sigz(m - 18)
            first_w = [(s_v, ms[("v_xn", i)])] if m == 0 else []
            if m < 2:
                bank_w = [(s_s, M("s_sigz", i - 1, 12 + m))] if i > 0 else []
            elif m < CE + 2:
                bank_w = [(s_s, M("s_evac", i, m - 2))]
            else:
                c = m - 16
                if c < 4:
                    bank_w = [(s_s, ms[("s_silu", i, 12 + c)])]
                else:
                    bank_w = [(s_s, M("s_sigz", i, c - 4))]
            if m < CE:
                dst, half = pp[1], m % 2
            else:
                dst, half = zbank(m - 16)
            for k in range(KD):
                A("tensor", lambda e, m=m, k=k, sl=sl, dst=dst, half=half: e.matmul(
                    dst.ap()[:, half * 512:(half + 1) * 512],
                    winb.ap()[:, k * 4096 + m * 128:k * 4096 + (m + 1) * 128],
                    xn[sl].ap()[:, k * T:(k + 1) * T],
                    start=(k == 0), stop=(k == KD - 1)),
                  waits=(first_w + bank_w
                         + [(s_winp[m // 4], 16)]) if k == 0 else [],
                  inc=(s_t if k == KD - 1 else None))
            ms[("t_pz", i, m)] = s_t.count
            if m < CE:
                hw = [(s_v, M("v_halo", i))] if i > 0 else []
                A("scalar", lambda e, m=m: e.copy(
                    xp.ap()[:, m * (T + HALO) + HALO:(m + 1) * (T + HALO)],
                    pp[1].ap()[:, (m % 2) * 512:(m % 2 + 1) * 512]),
                  waits=[(s_t, ms[("t_pz", i, m)]), (s_t, M("t_conv", i - 1, m))] + hw,
                  inc=s_s)
                ms[("s_evac", i, m)] = s_s.count
            if 0 <= m - 2 < CE:
                emit_conv(m - 2)
                emit_silu(m - 2)
            if 0 <= m - 3 < CE:
                emit_xs2(m - 3)
            if 10 <= m <= 17:
                emit_x2(i + 2, [m - 10])
            if i > 0 and m < CE:
                emit_chain_c(i - 1, m)
            if 0 <= m - 4 < CE:
                emit_sv(m - 4)
            if 17 <= m <= 20:
                emit_xn_part(i + 1, m - 17)
            if 0 <= m - 12 < KD:
                emit_s1k(i + 2, m - 12)
            if m == 21:
                vp2_vec(i)
            if m == 23:
                vp1_vec(i + 2)
        emit_sigz(14)
        emit_sigz(15)
        if debug and i == 0:
            for nm, buf, wl in (
                ("dbg_xn", xn[0], [(s_v, ms[("v_xn", 0)])]),
                ("dbg_xp", xp, [(s_s, ms[("s_evac", 0, CE - 1)])]),
                ("dbg_sigz", sigz, [(s_s, ms[("s_sigz", 0, CE - 1)])]),
                ("dbg_xs", xs, [(s_s, ms[("s_silu", 0, CE - 1)])]),
                ("dbg_yg", yg, [(s_v, ms[("v_chain", 0)])]),
            ):
                shp = [buf.shape[0], buf.shape[1]]
                dt_ = nc.dram_tensor(nm, shp, buf.dtype, kind="ExternalOutput").ap()
                A("sync", lambda e, d=dt_, b=buf: e.dma_start(d, b.ap()[:, :]),
                  waits=wl, inc=s_dbg, inc_n=16)

    p1_pz_done = ms[("t_pz", NT - 1, 2 * CE - 1)]
    p1_conv_done = ms[("t_conv", NT - 1, CE - 1)]
    p1_s2b_done = ms[("t_s2b", NT - 1, CE - 1)]
    p1_evac_done = ms[("s_evac", NT - 1, CE - 1)]
    p1_sigz_done = ms[("s_sigz", NT - 1, CE - 1)]
    p1_var2_done = ms[("v_var2", NT - 1)]
    NP2 = 384
    E1 = [(s_t, p1_s2b_done), (s_s, p1_evac_done)]
    E2 = [(s_t, p1_conv_done), (s_v, p1_var2_done)]
    A("sync", lambda e: e.dma_start(xn2[0].ap()[:, :], scr_xn[0]),
      waits=E2 + [(s_sxn[0], 16)], inc=s_xn2[0], inc_n=16)
    ms[("ld_xn2", 0)] = (s_xn2[0], s_xn2[0].count)
    for j in range(8):
        A("gpsimd", lambda e, j=j: e.dma_start(
            wgt.ap().rearrange("p (k n) -> p k n", k=KD)[:, :, j * NP2:(j + 1) * NP2],
            wgt_d[:, j * NP2:(j + 1) * NP2].rearrange("(k p) n -> p k n", p=128)),
          waits=E1, inc=s_wgp[j], inc_n=16)
    for j in range(8):
        A("gpsimd", lambda e, j=j: e.dma_start(
            wut.ap().rearrange("p (k n) -> p k n", k=KD)[:, :, j * NP2:(j + 1) * NP2],
            wut_d[:, j * NP2:(j + 1) * NP2].rearrange("(k p) n -> p k n", p=128)),
          waits=[(s_t, p1_pz_done)] + MP("st_xn", NT - 1), inc=s_wup[j], inc_n=16)
    vp2_scl(NT - 1)
    p1_r2_done = ms[("s_r2", NT - 1)]

    first2 = {"tensor": [(s_s, ms[("s_r1", NT - 1)])],
              "tensor_u": [(s_s, p1_sigz_done)],
              "vector": [], "scalar": [], "sync": []}

    def emit_sg(i, m):
        stw = [(s_shA, 16 * i)] if m < 12 else [(s_shB, 16 * i)]
        A("scalar", lambda e, m=m: e.activation(
            hbuf.ap()[:, m * T:(m + 1) * T],
            pp[0].ap()[:, (m % 2) * 512:(m % 2 + 1) * 512], AF.Silu, scale=S127),
          waits=first2["scalar"] + [(s_t, ms[("t_g", i, m)])]
                + [(s, v) for (s, v) in stw if v > 0], inc=s_s)
        first2["scalar"] = []
        ms[("s_sg", i, m)] = s_s.count

    def emit_vh(i, m):
        A("vector", lambda e, m=m: e.scalar_tensor_tensor(
            hbuf.ap()[:, m * T:(m + 1) * T], hbuf.ap()[:, m * T:(m + 1) * T],
            s2c.ap()[:, :1], pp[1].ap()[:, (m % 2) * 512:(m % 2 + 1) * 512],
            OP.mult, OP.mult),
          waits=first2["vector"] + [(s_s, ms[("s_sg", i, m)]), (s_t, ms[("t_u", i, m)])]
                + ([(s_v, M("v_chainc", NT - 1, m - 8))]
                   if (i == 0 and m >= 8) else []),
          inc=s_v)
        first2["vector"] = []
        ms[("v_h", i, m)] = s_v.count

    for i in range(NT):
        sl = i % 2
        if i > 0:
            A("gpsimd", lambda e, i=i, sl=sl: e.dma_start(
                xn2[sl].ap()[:, :], scr_xn[i]),
              waits=[(s_t, M("t_u", i - 2, CH - 1)), (s_sxn[sl], 16 * (i // 2 + 1))]
                    + (E2 if i < 2 else []),
              inc=s_xn2[sl], inc_n=16)
            ms[("ld_xn2", i)] = (s_xn2[sl], s_xn2[sl].count)

        if 2 <= i <= 4:
            jj = i - 2
            E4 = [(s_v, M("v_chain", NT - 1)), (s_syg, 16 * NT)] \
                 + ([(s_s, p1_r2_done)] if jj == 2 else [])
            A("gpsimd", lambda e, jj=jj: e.dma_start(
                wdtp[jj].ap()[:, :],
                wdt_d[jj * 768:(jj + 1) * 768, :].rearrange("(k p) n -> p k n", p=128)),
              waits=E4, inc=s_wdp[jj], inc_n=16)
            if jj == 0:
                A("scalar", lambda e: e.dma_start(
                    h3[0].ap()[:, :], scr_h[0]),
                  waits=[(s_v, M("v_chain", NT - 1)), (s_syg, 16 * NT),
                         (s_shA, 16), (s_shB, 16)],
                  inc=s_h3[0], inc_n=16)
                ms[("ld_h3", 0)] = (s_h3[0], s_h3[0].count)

        def emit_gate(m):
            gw = ([ms[("ld_xn2", i)]] if m == 0 else []) + [(s_wgp[m // 3], 16)] \
                 + [(s_s, M("s_sg", i, m - 2) if m >= 2 else M("s_sg", i - 1, CH - 2 + m))]
            for k in range(KD):
                A("tensor", lambda e, m=m, k=k, sl=sl: e.matmul(
                    pp[0].ap()[:, (m % 2) * 512:(m % 2 + 1) * 512],
                    wgt.ap()[:, k * 3072 + m * 128:k * 3072 + (m + 1) * 128],
                    xn2[sl].ap()[:, k * T:(k + 1) * T],
                    start=(k == 0), stop=(k == KD - 1)),
                  waits=(first2["tensor"] + gw) if k == 0 else [],
                  inc=(s_t if k == KD - 1 else None))
            first2["tensor"] = []
            ms[("t_g", i, m)] = s_t.count

        def emit_up(m):
            uw = ((first2["tensor_u"] if i == 0 else []) if m == 0 else []) \
                 + [(s_wup[m // 3], 16)] \
                 + [(s_v, M("v_h", i, m - 2) if m >= 2 else M("v_h", i - 1, CH - 2 + m))]
            for k in range(KD):
                A("tensor", lambda e, m=m, k=k, sl=sl: e.matmul(
                    pp[1].ap()[:, (m % 2) * 512:(m % 2 + 1) * 512],
                    wut.ap()[:, k * 3072 + m * 128:k * 3072 + (m + 1) * 128],
                    xn2[sl].ap()[:, k * T:(k + 1) * T],
                    start=(k == 0), stop=(k == KD - 1)),
                  waits=uw if k == 0 else [], inc=(s_t if k == KD - 1 else None))
            ms[("t_u", i, m)] = s_t.count

        if i == 0 or i == NT - 1:
            for m in range(CH):
                if i == 0 and m < CE:
                    emit_chain_c(NT - 1, m)
                emit_gate(m)
                emit_sg(i, m)
            for m in range(CH):
                emit_up(m)
                emit_vh(i, m)
                if m == 11:
                    A("sync", lambda e, i=i: e.dma_start(
                        scr_h[i][:, 0:12 * T], hbuf.ap()[:, 0:12 * T]),
                      waits=[(s_v, ms[("v_h", i, 11)])], inc=s_shA, inc_n=16)
                    ms[("st_hA", i)] = (s_shA, s_shA.count)
            A("sync", lambda e, i=i: e.dma_start(
                scr_h[i][:, 12 * T:], hbuf.ap()[:, 12 * T:]),
              waits=[(s_v, ms[("v_h", i, CH - 1)])], inc=s_shB, inc_n=16)
            ms[("st_hB", i)] = (s_shB, s_shB.count)
            continue

        for m in range(CH):
            emit_gate(m)
            emit_up(m)
            emit_sg(i, m)
            emit_vh(i, m)
            if m == 11:
                A("sync", lambda e, i=i: e.dma_start(
                    scr_h[i][:, 0:12 * T], hbuf.ap()[:, 0:12 * T]),
                  waits=[(s_v, ms[("v_h", i, 11)])], inc=s_shA, inc_n=16)
                ms[("st_hA", i)] = (s_shA, s_shA.count)
        A("sync", lambda e, i=i: e.dma_start(
            scr_h[i][:, 12 * T:], hbuf.ap()[:, 12 * T:]),
          waits=[(s_v, ms[("v_h", i, CH - 1)])], inc=s_shB, inc_n=16)
        ms[("st_hB", i)] = (s_shB, s_shB.count)

    p2_g_done = ms[("t_g", NT - 1, CH - 1)]
    p2_u_done = ms[("t_u", NT - 1, CH - 1)]
    p2_h_done = ms[("v_h", NT - 1, CH - 1)]
    p2_sg_done = ms[("s_sg", NT - 1, CH - 1)]
    shA_total = s_shA.count
    shB_total = s_shB.count
    syg_total = s_syg.count

    for j in range(2):
        A("gpsimd", lambda e, j=j: e.dma_start(
            woutb.ap().rearrange("p (k n) -> p k n", k=CE)[:, j * 8:(j + 1) * 8, :],
            wout[j * 1024:(j + 1) * 1024, :].rearrange("(k p) n -> p k n", p=128)),
          waits=[(s_t, p2_g_done)], inc=s_wop[j], inc_n=16)
    A("gpsimd", lambda e: e.dma_start(
        wdtp[3].ap()[:, :],
        wdt_d[3 * 768:4 * 768, :].rearrange("(k p) n -> p k n", p=128)),
      waits=[(s_t, p2_g_done)], inc=s_wdp[3], inc_n=16)

    first3 = {"tensor": [(s_s, p2_sg_done), (s_v, p2_h_done)],
              "vector": [(s_shA, shA_total), (s_shB, shB_total)], "sync": []}

    for i in range(NT):
        sl = i % 2
        if i > 0:
            A("gpsimd", lambda e, i=i, sl=sl: e.dma_start(h3[sl].ap()[:, :], scr_h[i]),
              waits=[(s_t, M("t_down", i - 2, 3)),
                     (s_shA, min(16 * (i + 1), shA_total)), (s_shB, 16 * (i + 1))]
                    + ([(s_t, p2_u_done)] if i < 3 else []),
              inc=s_h3[sl], inc_n=16)
            ms[("ld_h3", i)] = (s_h3[sl], s_h3[sl].count)
        A("gpsimd", lambda e, i=i, sl=sl: e.dma_start(
            x3[sl].ap().rearrange("p (s d) -> p s d", s=4),
            xnat[i * T:(i + 1) * T, :].rearrange("(s p) d -> p s d", p=128)),
          waits=[(s_v, M("v_evac3", i - 2, 3))]
                + ([[(s_t, p2_u_done)], [(s_v, p2_h_done), (s_shA, shA_total)]][i]
                   if i < 2 else []),
          inc=s_x3[sl], inc_n=16)
        ms[("ld_x3", i)] = (s_x3[sl], s_x3[sl].count)
        A("gpsimd", lambda e, i=i: e.dma_start(yg3.ap()[:, :], scr_yg[i]),
          waits=[(s_t, M("t_acc", i - 1, 3)), (s_syg, 16 * (i + 1))]
                + ([(s_t, p2_u_done)] if i == 0 else []),
          inc=s_yg3, inc_n=16)
        ms[("ld_yg3", i)] = (s_yg3, s_yg3.count)

        dorder = ([(k, s) for k in range(CH) for s in range(4)] if i == 0
                  else [(k, s) for s in range(4) for k in range(CH)])
        for k, s in dorder:
            w0 = ([(s_wdp[k // 6], 16)] if k % 6 == 0 else [])
            if k == 0:
                w0 = [ms[("ld_h3", i)], (s_v, M("v_evac3", i - 1, s))] + w0
                if i == 0 and s == 0:
                    w0 = first3["tensor"] + w0
            for half in range(2):
                A("tensor", lambda e, s=s, k=k, half=half, sl=sl: e.matmul(
                    pp[s].ap()[:, half * 512:(half + 1) * 512],
                    h3[sl].ap()[:, k * T + s * 128:k * T + (s + 1) * 128],
                    wdtp[k // 6].ap()[:, (k % 6) * 1024 + half * 512:(k % 6) * 1024 + (half + 1) * 512],
                    start=(k == 0), stop=False, skip_group_check=True),
                  waits=w0 if half == 0 else [],
                  inc=(s_t if (k == CH - 1 and half == 1) else None))
                w0 = []
            if k == CH - 1:
                ms[("t_down", i, s)] = s_t.count
        aorder = ([(k, s) for k in range(CE) for s in range(4)] if i == 0
                  else [(k, s) for s in range(4) for k in range(CE)])
        for k, s in aorder:
            w0 = ([ms[("ld_yg3", i)]] if k == 0 else []) \
                 + ([(s_wop[k // 8], 16)] if k % 8 == 0 else [])
            for half in range(2):
                A("tensor", lambda e, s=s, k=k, half=half: e.matmul(
                    pp[s].ap()[:, half * 512:(half + 1) * 512],
                    yg3.ap()[:, k * T + s * 128:k * T + (s + 1) * 128],
                    woutb.ap()[:, k * 1024 + half * 512:k * 1024 + (half + 1) * 512],
                    start=False, stop=(k == CE - 1), skip_group_check=True),
                  waits=w0 if half == 0 else [],
                  inc=(s_t if (k == CE - 1 and half == 1) else None))
                w0 = []
            if k == CE - 1:
                ms[("t_acc", i, s)] = s_t.count
        for s in range(4):
            A("vector", lambda e, s=s, sl=sl: e.tensor_add(
                outb.ap()[:, s * 1024:(s + 1) * 1024], pp[s].ap()[:, :],
                x3[sl].ap()[:, s * 1024:(s + 1) * 1024]),
              waits=first3["vector"] + [(s_t, ms[("t_acc", i, s)]),
                                        ms[("ld_x3", i)]]
                    + MP("st_out", i - 1, s),
              inc=s_v)
            first3["vector"] = []
            ms[("v_evac3", i, s)] = s_v.count
            A("sync", lambda e, i=i, s=s: e.dma_start(
                out_d[i * T + s * 128:i * T + (s + 1) * 128, :],
                outb.ap()[:, s * 1024:(s + 1) * 1024]),
              waits=first3["sync"] + [(s_v, ms[("v_evac3", i, s)])],
              inc=s_sout, inc_n=16)
            first3["sync"] = []
            ms[("st_out", i, s)] = (s_sout, s_sout.count)

    final_st = s_sout.count
    A("sync", lambda e: e.wait_ge(s_sout.h, final_st))

    with nc.Block() as block:
        block.tensor(_runner(sched, "tensor"))
        block.vector(_runner(sched, "vector"))
        block.scalar(_runner(sched, "scalar"))
        block.sync(_runner(sched, "sync"))
        block.gpsimd(_runner(sched, "gpsimd"))
    es.close()
    return nc


_NC_CACHE = {}


def kernel(L0_attn_x, L0_attn_Win, L0_attn_conv_w, L0_attn_conv_b, L0_attn_Wx,
           L0_attn_Wdt, L0_attn_bdt, L0_attn_A_log, L0_attn_D, L0_attn_ln_g,
           L0_attn_ln_b, L0_attn_Wout, L0_attn_res_scale, L0_ffn_w_gate,
           L0_ffn_w_up, L0_ffn_w_down, **_unused):
    x = np.asarray(L0_attn_x, dtype=np.float32)
    Win = np.ascontiguousarray(np.asarray(L0_attn_Win, np.float32)[:, :4096])
    Wout = np.ascontiguousarray(np.asarray(L0_attn_Wout, np.float32))
    wgT = np.ascontiguousarray(np.asarray(L0_ffn_w_gate).T).astype(BF)
    wuT = np.ascontiguousarray(np.asarray(L0_ffn_w_up).T).astype(BF)
    wdT = np.ascontiguousarray(np.asarray(L0_ffn_w_down).T).astype(BF)
    D = np.asarray(L0_attn_D, np.float32)
    cwv = np.asarray(L0_attn_conv_w, np.float32)
    cbv = np.asarray(L0_attn_conv_b, np.float32)
    lngv = np.asarray(L0_attn_ln_g, np.float32)
    lnbv = np.asarray(L0_attn_ln_b, np.float32)
    rsv = np.asarray(L0_attn_res_scale, np.float32)

    dcol = np.ascontiguousarray(D.reshape(CE, 128).T)
    cw = np.ascontiguousarray(cwv.reshape(CE, 128, 4).transpose(1, 0, 2).reshape(128, CE * 4))
    cb = np.ascontiguousarray(cbv.reshape(CE, 128).T)
    lng = np.ascontiguousarray(lngv.reshape(CE, 128).T)
    lnb = np.ascontiguousarray(lnbv.reshape(CE, 128).T)
    rs = np.full((128, 1), rsv[0], np.float32)

    in_maps = []
    for c in range(8):
        b, hh = c // 2, c % 2
        s0 = hh * NTOK
        xseq = x[b]
        xTh = np.zeros((1024, HALO + NTOK), np.float32)
        xTh[:, HALO:] = xseq[s0:s0 + NTOK].T
        if s0 > 0:
            xTh[:, HALO - 3:HALO] = xseq[s0 - 3:s0].T
        in_maps.append({
            "xTh": np.ascontiguousarray(xTh),
            "xnat": np.ascontiguousarray(xseq[s0:s0 + NTOK]),
            "win": Win, "wout": Wout, "wgt": wgT, "wut": wuT, "wdt": wdT,
            "dcol": dcol, "cw": cw, "cb": cb, "lng": lng, "lnb": lnb, "rs": rs,
        })

    nc = build_nc()
    res = run_bass_kernel_spmd(nc, in_maps, core_ids=list(range(8)))
    out = np.empty((4, 8192, 1024), np.float32)
    for c in range(8):
        b, hh = c // 2, c % 2
        out[b, hh * NTOK:(hh + 1) * NTOK] = res.results[c]["out"]
    return out


if __name__ == "__main__":
    rng = np.random.RandomState(0)
    print("smoke build only")
    build_nc()
    print("build ok")
